# version 23
# baseline (speedup 1.0000x reference)
import numpy as np
import ml_dtypes
from contextlib import ExitStack

import concourse.bass as bass
import concourse.mybir as mybir
from concourse.bass_utils import run_bass_kernel_spmd

F32 = mybir.dt.float32
BF16 = mybir.dt.bfloat16
AF = mybir.ActivationFunctionType
ALU = mybir.AluOpType
AX = mybir.AxisListType
BFDT = ml_dtypes.bfloat16

T = 2048
C = 1024
NT = 16
NB = 4
H_O, D = 16, 64
KB_O = 3
KW_O = KB_O * 128
H_F, CF = 8, 512
KB_F = 5
KW_F = KB_F * 128
NPAIR = 22
W1C = NPAIR * 256
W2R = NPAIR * 128
SPLITS = [range(0, 5), range(5, 9), range(9, 14), range(14, 18), range(18, 22)]
NEG = -1.0e30

DMA_ENGS = ("Sy", "G")
DEBUG = False


class Task:
    __slots__ = ("eng", "emit", "reads", "writes", "inc", "deps", "flag", "cum",
                 "waits", "tag")

    def __init__(self, eng, emit, reads, writes, inc, tag):
        self.eng = eng
        self.emit = emit
        self.reads = reads
        self.writes = writes
        self.inc = inc
        self.deps = set()
        self.flag = eng in DMA_ENGS
        self.cum = 0
        self.waits = {}
        self.tag = tag


class Prog:
    def __init__(self):
        self.tasks = []
        self.last_write = {}
        self.readers = {}
        self.floor = None

    def add(self, eng, emit, reads=(), writes=(), ndma=1, tag=""):
        inc = 16 * ndma if eng in DMA_ENGS else 1
        t = Task(eng, emit, list(reads), list(writes), inc, tag)
        i = len(self.tasks)
        for b in t.reads:
            w = self.last_write.get(b)
            if w is not None:
                t.deps.add(w)
            self.readers.setdefault(b, []).append(i)
        for b in t.writes:
            w = self.last_write.get(b)
            if w is not None:
                t.deps.add(w)
            for rd in self.readers.get(b, []):
                if rd != i:
                    t.deps.add(rd)
            self.last_write[b] = i
            self.readers[b] = []
        if self.floor:
            for eng_, j in self.floor.items():
                if eng_ != t.eng:
                    t.deps.add(j)
        self.tasks.append(t)
        return i

    def barrier(self):
        fl = {}
        for i, t in enumerate(self.tasks):
            fl[t.eng] = i
        self.floor = fl

    def finalize(self):
        tasks = self.tasks
        for t in tasks:
            for j in t.deps:
                if tasks[j].eng != t.eng or t.eng != "T":
                    tasks[j].flag = True
        cnt = {}
        for t in tasks:
            if t.flag:
                cnt[t.eng] = cnt.get(t.eng, 0) + t.inc
            t.cum = cnt.get(t.eng, 0)
        for t in tasks:
            for j in t.deps:
                tj = tasks[j]
                if tj.eng == t.eng and t.eng == "T":
                    continue
                if tj.eng == t.eng and tj.cum >= t.cum:
                    pass
                if tj.flag and tj.cum > t.waits.get(tj.eng, -1):
                    t.waits[tj.eng] = tj.cum
        self.final_counts = cnt


def _tok_major(tab):
    return np.ascontiguousarray(
        tab.reshape(NT, 128, 64).transpose(1, 0, 2).reshape(128, NT * 64))


def make_tables():
    t64 = np.arange(T, dtype=np.float64)
    k = np.arange(0, D, 2, dtype=np.float64)
    inv = 10000.0 ** (-(k / D))
    fr = np.outer(t64, inv)
    f2 = np.concatenate([fr, fr], axis=1)
    sign = np.where(np.arange(D) < 32, -1.0, 1.0)[None, :]
    C2 = np.cos(f2)
    S2 = sign * np.sin(f2)
    half = np.arange(0, D, 2, dtype=np.float64)
    invf = 10000.0 ** (-(half / D))
    frf = np.outer(t64, invf)
    ff = np.concatenate([frf, frf], axis=1)
    base = (half + 0.4 * D) / (1.4 * D)
    power = (t64 - T // 2) / 256.0
    sc = base[None, :] ** power[:, None]
    sc2 = np.concatenate([sc, sc], axis=1)
    CXq = np.cos(ff) * sc2
    SXq = sign * np.sin(ff) * sc2
    CXk = np.cos(ff) / sc2
    SXk = sign * np.sin(ff) / sc2
    f32 = lambda a: _tok_major(a).astype(np.float32)
    return f32(C2), f32(S2), f32(CXq), f32(SXq), f32(CXk), f32(SXk)


def make_masks():
    p = np.arange(128)[:, None]
    co = np.arange(KW_O)[None, :]
    mo = [np.where((co >= p) & (co <= p + 256) & (co >= 256 - 128 * t), 0.0, NEG)
          for t in range(KB_O)]
    cf = np.arange(KW_F)[None, :]
    mf = [np.where((cf >= p) & (cf <= p + 512) & (cf >= 512 - 128 * t), 0.0, NEG)
          for t in range(KB_F)]
    return (np.concatenate(mo, axis=0).astype(BFDT),
            np.concatenate(mf, axis=0).astype(BFDT))


def permute_w1(w1):
    blocks = []
    a, g = w1[:, :2730], w1[:, 2730:]
    for c in range(NPAIR):
        ab = np.zeros((1024, 128), np.float32)
        gb = np.zeros((1024, 128), np.float32)
        n = min(128, 2730 - c * 128)
        ab[:, :n] = a[:, c * 128:c * 128 + n]
        gb[:, :n] = g[:, c * 128:c * 128 + n]
        blocks += [ab, gb]
    return np.concatenate(blocks, axis=1)


def permute_b1(b1):
    a, g = b1[:2730], b1[2730:]
    cols = np.zeros((128, 2 * NPAIR), np.float32)
    for c in range(NPAIR):
        n = min(128, 2730 - c * 128)
        cols[:n, 2 * c] = a[c * 128:c * 128 + n]
        cols[:n, 2 * c + 1] = g[c * 128:c * 128 + n]
    return cols


def pad_w2(w2):
    out = np.zeros((W2R, 1024), np.float32)
    out[:2730] = w2
    return out



def _ld_chunks(q, dst, src_ap, nchunks, ncols):
    d = bass.AP(dst, 0, [[dst.shape[1], 128], [ncols, nchunks], [1, ncols]])
    s = src_ap.rearrange("(kc p) n -> p kc n", p=128)
    return q.dma_start(d, s)


def build(nb=NB):
    nc = bass.Bass("TRN2", target_bir_lowering=False)
    dram = {}

    def din(name, shape, dt=F32):
        dram[name] = nc.dram_tensor(name, shape, dt, kind="ExternalInput").ap()
        return dram[name]

    x_d = din("x", [nb * T, C])
    wqt_d = din("wqt", [C, 3 * C])
    wot_d = din("wot", [C, C])
    fqkv_d = din("fqkv", [2 * C, 3 * CF])
    fwo_d = din("fwo", [2 * CF, C])
    fw1_d = din("fw1", [2 * C, W1C])
    fw2_d = din("fw2", [2 * W2R, C])
    fb1_d = din("fb1", [2 * 128, 2 * NPAIR])
    fb2_d = din("fb2", [2, C])
    lgb_d = din("lgb", [2 * 128, C])
    lbb_d = din("lbb", [2 * 128, C])
    qsb_d = din("qsb", [2 * 128, D])
    ksb_d = din("ksb", [2 * 128, D])
    tc2_d = din("tc2", [128, NT * D]); ts2_d = din("ts2", [128, NT * D])
    tcx_d = {("q", "c"): din("tcxq", [128, NT * D]),
             ("q", "s"): din("tsxq", [128, NT * D]),
             ("k", "c"): din("tcxk", [128, NT * D]),
             ("k", "s"): din("tsxk", [128, NT * D])}
    mko_d = din("mko", [KB_O * 128, KW_O], BF16)
    mkf_d = din("mkf", [KB_F * 128, KW_F], BF16)
    id_d = din("ident", [128, 128], BF16)
    shm_d = din("shm", [128, 128], BF16)
    e127_d = din("e127", [128, 128], BF16)
    shmf_d = din("shmf", [128, 128], F32)
    e127f_d = din("e127f", [128, 128], F32)
    ones1_d = din("ones1", [1, 128], BF16)

    out_d = nc.dram_tensor("out", [nb * T, C], F32, kind="ExternalOutput").ap()
    x1p_d = nc.dram_tensor("x1p", [T, C], F32).ap()

    dbg_seen = set()

    def dbg(name, ap_fn, shape, dt, reads):
        if not DEBUG or name in dbg_seen:
            return
        dbg_seen.add(name)
        dd = nc.dram_tensor("dbg_" + name, shape, dt, kind="ExternalOutput").ap()

        def em(q):
            return [q.dma_start(dd, ap_fn())]
        P.add("Sy", em, reads=reads, writes=["dbg" + name], ndma=1, tag="dbg" + name)

    P = Prog()
    root = ExitStack()

    def mk(stack, name, shape, dt):
        return stack.enter_context(nc.sbuf_tensor(name, shape, dt))

    xf = mk(root, "xf", [128, NT * C], F32)
    Pb = mk(root, "Pb", [128, 4 * KW_O], BF16)
    PbT = mk(root, "PbT", [128, 4 * KW_O], BF16)
    ob = mk(root, "ob", [128, C], BF16)
    oT = mk(root, "oT", [128, C], BF16)
    stat = mk(root, "stat", [128, 32], F32)
    statc = [0]
    rmx = mk(root, "rmx", [128, 16], F32)
    ngm = mk(root, "ngm", [128, 16], F32)
    zc = mk(root, "zc", [128, 16], F32)
    rzc = mk(root, "rzc", [128, 16], F32)
    ssq8 = mk(root, "ssq8", [128, 16], F32)
    ssk8 = mk(root, "ssk8", [128, 16], F32)
    cq8 = mk(root, "cq8", [128, 8], F32)
    rk8 = mk(root, "rk8", [128, 16], F32)
    sq8a = mk(root, "sq8a", [128, 16], F32)
    sq8b = mk(root, "sq8b", [128, 16], F32)
    idm = mk(root, "idm", [128, 128], BF16)
    shmF = mk(root, "shmF", [128, 128], F32)
    e127F = mk(root, "e127F", [128, 128], F32)
    ones1 = mk(root, "ones1S", [1, 128], BF16)
    rsc1 = mk(root, "rsc1", [128, C], F32)
    rsc2 = mk(root, "rsc2", [128, C], F32)
    rsc4 = mk(root, "rsc4", [128, C], F32)
    shb = mk(root, "shb", [128, C], BF16)
    aT = mk(root, "aT", [128, C], BF16)
    qz = mk(root, "qz", [128, H_O * 128], BF16)

    PA = root.enter_context(nc.psum_tensor("PA", [128, 1024], F32))
    PB = root.enter_context(nc.psum_tensor("PB", [128, 1024], F32))
    PC = root.enter_context(nc.psum_tensor("PC", [128, 512], F32))
    PD = root.enter_context(nc.psum_tensor("PD", [128, 512], F32))
    PT = root.enter_context(nc.psum_tensor("PT", [128, 1024], BF16))
    PT2 = root.enter_context(nc.psum_tensor("PT2", [128, 1024], BF16))
    PTs = [(PT, "PT"), (PT2, "PT2")]
    ptc = [0]

    def pt_next():
        ptc[0] ^= 1
        return PTs[ptc[0]]

    def bankset(nm, lo, hi):
        s = []
        if lo < 512:
            s.append(nm + "0")
        if hi > 512:
            s.append(nm + "1")
        return s

    def em_ldc2(q):
        return [q.dma_start(idm[:], id_d), q.dma_start(shmF[:], shmf_d),
                q.dma_start(e127F[:], e127f_d), q.dma_start(ones1[:], ones1_d)]
    P.add("G", em_ldc2, writes=["idm", "shmF", "e127F", "ones1"], ndma=4, tag="ldc2")

    def em_qz0(v):
        return [v.memset(qz[:], 0.0)]
    P.add("V", em_qz0, writes=["qzD"], tag="qz0")

    def ln_stats(pref, src_aps, nfree, extra_rms):
        statc[0] ^= 1
        so_ = statc[0] * 16
        sname = f"stat{statc[0]}"
        sm = stat[:, so_ + 0:so_ + 1]; ssc = stat[:, so_ + 1:so_ + 2]
        t1 = stat[:, so_ + 2:so_ + 3]; aa = stat[:, so_ + 3:so_ + 4]
        bb = stat[:, so_ + 4:so_ + 5]; lnv = stat[:, so_ + 5:so_ + 6]
        r1 = stat[:, so_ + 6:so_ + 7]; nmc = stat[:, so_ + 7:so_ + 8]
        tr_ = stat[:, so_ + 8:so_ + 9]; tq = stat[:, so_ + 9:so_ + 10]
        mean = stat[:, so_ + 10:so_ + 11]
        rnames = [n for _, n, _ in src_aps]
        c1 = ((1.0 + 1.0e-6) if extra_rms else 1.0) / nfree
        c0 = 1.0e-11 if extra_rms else 1.0e-5

        def em_v1(v):
            return [v.reduce_sum(tr_ if i else sm, ap, axis=AX.X)
                    for i, (ap, _, _) in enumerate(src_aps)]
        P.add("V", em_v1, reads=rnames + [sname], writes=[sname], tag=pref + "v1")

        def em_s1(s):
            return [s.activation(rsc4[:, 0:w], ap, AF.Square,
                                 accum_out=(tq if i else ssc))
                    for i, (ap, _, w) in enumerate(src_aps)]
        P.add("S", em_s1, reads=rnames + [sname], writes=[sname, "rs4"],
              tag=pref + "s1")

        if len(src_aps) > 1:
            def em_v1b(v):
                return [v.tensor_add(sm, sm, tr_), v.tensor_add(ssc, ssc, tq)]
            P.add("V", em_v1b, reads=[sname], writes=[sname], tag=pref + "v1b")

        def em_v2(v):
            return [v.scalar_tensor_tensor(t1, sm, 1.0 / nfree, sm,
                                           ALU.mult, ALU.mult)]
        P.add("V", em_v2, reads=[sname], writes=[sname], tag=pref + "v2")

        def em_v3(v):
            return [v.tensor_sub(aa, ssc, t1)]
        P.add("V", em_v3, reads=[sname], writes=[sname], tag=pref + "v3")

        def em_v4(v):
            return [v.tensor_scalar(bb, aa, c1, c0, ALU.mult, ALU.add)]
        P.add("V", em_v4, reads=[sname], writes=[sname], tag=pref + "v4")

        def em_s2(s):
            return [s.activation(lnv, bb, AF.Ln)]
        P.add("S", em_s2, reads=[sname], writes=[sname], tag=pref + "s2")

        def em_s3(s):
            return [s.activation(r1, lnv, AF.Exp, scale=-0.5)]
        P.add("S", em_s3, reads=[sname], writes=[sname], tag=pref + "s3")

        def em_v5(v):
            r = [v.scalar_tensor_tensor(nmc, sm, -1.0 / nfree, r1,
                                        ALU.mult, ALU.mult)]
            if not extra_rms:
                r.append(v.tensor_scalar_mul(mean, sm, 1.0 / nfree))
            return r
        P.add("V", em_v5, reads=[sname], writes=[sname], tag=pref + "v5")
        return r1, nmc, mean, sname

    def attention_tile(pref, t, nheads, kb, kw, kroll, kn_pref, vroll, vn_pref,
                       nslots, vwidth, mask_sb, mask_name, o_ps_fn, o_names,
                       exp_scale, cq_fn, full_kv=False, mid_hook=None,
                       qz_off=0, qz_name="qzD", hooks=None, cq_name="cq8"):
        teff = min(t, kb - 1)
        kts = [t - (kb - 1) + j for j in range(kb)]
        pbd = 4 if kw <= 512 else 2
        if full_kv:
            kreads = ["kTF"]
            vreads = ["vF"]
            spc_full = nslots
        else:
            kreads = [kn_pref + str(kt % nslots) for kt in kts if kt >= 0]
            vreads = [vn_pref + str(kt % nslots) for kt in kts if kt >= 0]
            spc_full = nslots + kb - 1

        if kw <= 512:
            slots = [(PA, 0, ["PA0"]), (PA, 512, ["PA1"]),
                     (PB, 0, ["PB0"]), (PB, 512, ["PB1"])]
        else:
            slots = [(PA, 0, ["PA0", "PA1"]), (PB, 0, ["PB0", "PB1"])]
        depth = len(slots)

        def sc_slot(h):
            return slots[h % depth]

        def em_sc_f(h):
            sc_t, sc_o, sbanks = sc_slot(h)
            pc = h // 2

            def em_sc(te, h=h, pc=pc, sc_t=sc_t, sc_o=sc_o, qz_off=qz_off):
                r = []
                started = [False, False]
                if t >= kb - 1:
                    spc = spc_full
                    s0 = (t - (kb - 1)) % nslots
                    base = (pc * spc + s0) * 128
                    n1 = min(kw, 512)
                    r.append(te.matmul(sc_t[:, sc_o:sc_o + n1],
                                       qz[:, qz_off + h * 128:qz_off + (h + 1) * 128],
                                       kroll[:, base:base + n1],
                                       start=True, stop=False,
                                       skip_group_check=True))
                    started[(sc_o) // 512] = True
                    if kw > 512:
                        r.append(te.matmul(sc_t[:, sc_o + 512:sc_o + kw],
                                           qz[:, qz_off + h * 128:qz_off + (h + 1) * 128],
                                           kroll[:, base + 512:base + kw],
                                           start=True, stop=False,
                                           skip_group_check=True))
                        started[1] = True
                else:
                    for j, kt in enumerate(kts):
                        if kt < 0:
                            continue
                        bk = (sc_o + j * 128) // 512
                        spc = spc_full
                        r.append(te.matmul(
                            sc_t[:, sc_o + j * 128:sc_o + (j + 1) * 128],
                            qz[:, qz_off + h * 128:qz_off + (h + 1) * 128],
                            kroll[:, (pc * spc + (kt % nslots)) * 128:
                                  (pc * spc + (kt % nslots)) * 128 + 128],
                            start=not started[bk], stop=False,
                            skip_group_check=True))
                        started[bk] = True
                o = teff * kw
                n0 = min(kw, 512)
                mb = sc_o // 512
                if t >= kb - 1:
                    dcol = (kb - 1) * 128
                    r.append(te.matmul(sc_t[:, sc_o:sc_o + 128], idm[:],
                                       mask_sb[:, o:o + 128],
                                       start=not started[mb], stop=False,
                                       skip_group_check=True))
                    r.append(te.matmul(sc_t[:, sc_o + dcol:sc_o + kw], idm[:],
                                       mask_sb[:, o + dcol:o + kw],
                                       start=not started[(sc_o + dcol) // 512],
                                       stop=True, skip_group_check=True))
                    return r
                r.append(te.matmul(sc_t[:, sc_o:sc_o + n0], idm[:],
                                   mask_sb[:, o:o + n0],
                                   start=not started[mb], stop=(kw <= 512),
                                   skip_group_check=True))
                if kw > 512:
                    r.append(te.matmul(sc_t[:, sc_o + 512:sc_o + kw], idm[:],
                                       mask_sb[:, o + 512:o + kw],
                                       start=not started[mb + 1], stop=True,
                                       skip_group_check=True))
                return r
            P.add("T", em_sc, reads=[qz_name, mask_name, "idm"] + kreads,
                  writes=sbanks, tag=f"{pref}sc{t}_{h}")

        def em_rmng_f(h):
            sc_t, sc_o, sbanks = sc_slot(h)

            def em_rm(v, h=h, sc_t=sc_t, sc_o=sc_o):
                return [v.reduce_max(rmx[:, h:h + 1], sc_t[:, sc_o:sc_o + kw],
                                     axis=AX.X)]
            P.add("V", em_rm, reads=sbanks, writes=["rmx"], tag=f"{pref}rm{t}_{h}")

            def em_ng(v, h=h):
                if cq_fn is None:
                    return [v.tensor_scalar_mul(ngm[:, h:h + 1], rmx[:, h:h + 1],
                                                -exp_scale)]
                return [v.scalar_tensor_tensor(ngm[:, h:h + 1], rmx[:, h:h + 1],
                                               -1.0, cq_fn(h), ALU.mult, ALU.mult)]
            P.add("V", em_ng, reads=["rmx"] + ([cq_name] if cq_fn else []),
                  writes=["ngm"], tag=f"{pref}ng{t}_{h}")

        def em_ex_f(h):
            sc_t, sc_o, sbanks = sc_slot(h)
            par = h % pbd
            po = par * kw

            def em_ex(s, h=h, sc_t=sc_t, sc_o=sc_o, po=po):
                scale = exp_scale if cq_fn is None else cq_fn(h)
                return [s.activation(Pb[:, po:po + kw], sc_t[:, sc_o:sc_o + kw],
                                     AF.Exp, bias=ngm[:, h:h + 1], scale=scale,
                                     accum_out=zc[:, h:h + 1])]
            P.add("S", em_ex, reads=sbanks + ["ngm"] + ([cq_name] if cq_fn else []),
                  writes=[f"Pbb{par}", "zc"], tag=f"{pref}ex{t}_{h}")

        def em_ptpe_f(h):
            par = h % pbd
            po = par * kw
            pt_, ptn = pt_next()

            def em_pt(te, po=po, pt_=pt_):
                return [te.transpose(pt_[:, j * 128:(j + 1) * 128],
                                     Pb[:, po + j * 128:po + (j + 1) * 128], idm[:])
                        for j, kt in enumerate(kts) if kt >= 0]
            P.add("T", em_pt, reads=[f"Pbb{par}", "idm"], writes=[ptn],
                  tag=f"{pref}pt{t}_{h}")

            def em_pe(v, po=po, pt_=pt_):
                return [v.tensor_copy(PbT[:, po:po + kw], pt_[:, 0:kw])]
            P.add("V", em_pe, reads=[ptn], writes=[f"PbTb{par}"],
                  tag=f"{pref}pe{t}_{h}")

        def em_pv_f(h):
            par = h % pbd
            po = par * kw

            def em_pv(te, h=h, po=po):
                r = []
                first = True
                for j, kt in enumerate(kts):
                    if kt < 0:
                        continue
                    r.append(te.matmul(
                        o_ps_fn(h),
                        PbT[:, po + j * 128:po + (j + 1) * 128],
                        vroll[:, (kt % nslots) * vwidth + h * 64:
                              (kt % nslots) * vwidth + h * 64 + 64],
                        start=first, stop=(j == kb - 1), skip_group_check=True))
                    first = False
                return r
            P.add("T", em_pv, reads=[f"PbTb{par}"] + vreads, writes=o_names,
                  tag=f"{pref}pv{t}_{h}")

        for h in range(min(depth, nheads)):
            em_sc_f(h)
        for h in range(min(pbd, nheads)):
            em_rmng_f(h)
            em_ex_f(h)
        for h in range(nheads):
            if h + depth < nheads:
                em_sc_f(h + depth)
            if h + pbd < nheads:
                em_rmng_f(h + pbd)
            em_ptpe_f(h)
            if h >= 1:
                em_pv_f(h - 1)
            if h + pbd < nheads:
                em_ex_f(h + pbd)
            if mid_hook is not None and h == nheads // 2:
                mid_hook()
            if hooks is not None and h in hooks:
                for fn in hooks[h]:
                    fn()
        em_pv_f(nheads - 1)

        def em_rz(v):
            return [v.reciprocal(rzc[:, 0:nheads], zc[:, 0:nheads])]
        P.add("V", em_rz, reads=["zc"], writes=["rzc"], tag=f"{pref}rz{t}")

    def stageB(b):
        st_ = ExitStack()
        WQ = mk(st_, f"WQ{b}", [128, 8 * 3 * C], BF16)
        WO = mk(st_, f"WO{b}", [128, 8 * C], BF16)
        kroll = mk(st_, f"kro{b}", [128, 8 * 6 * 128], BF16)
        vroll = mk(st_, f"vro{b}", [128, 4 * C], BF16)
        qnb = mk(st_, f"qnb{b}", [128, C], BF16)
        knb = mk(st_, f"knb{b}", [128, C], BF16)
        xtile = mk(st_, f"xt{b}", [128, 2 * C], F32)
        x1sb = mk(st_, f"x1b{b}", [128, C], F32)
        tC2 = mk(st_, f"tC2{b}", [128, NT * D], F32)
        tS2 = mk(st_, f"tS2{b}", [128, NT * D], F32)
        mko = mk(st_, f"mko{b}", [128, KB_O * KW_O], BF16)

        def em_w(q):
            return [_ld_chunks(q, WQ, wqt_d, 8, 3 * C),
                    _ld_chunks(q, WO, wot_d, 8, C)]
        P.add("G", em_w, writes=["WQ", "WO"], ndma=2, tag=f"b{b}w")

        def em_t(q):
            return [q.dma_start(tC2[:], tc2_d), q.dma_start(tS2[:], ts2_d),
                    _ld_chunks(q, mko, mko_d, KB_O, KW_O)]
        P.add("Sy", em_t, writes=["tC2", "tS2", "mko"], ndma=3, tag=f"b{b}t")

        def partA(t):
            xcur = xtile[:, (t % 2) * C:(t % 2) * C + C]
            xname = f"xt{t % 2}"
            def em_x(q, t=t, xcur=xcur):
                return [q.dma_start(xcur, x_d[b * T + t * 128:b * T + (t + 1) * 128, :])]
            P.add("Sy", em_x, writes=[xname], ndma=1, tag=f"b{b}x{t}")

            comb, nmc, _, sn1 = ln_stats(f"b{b}n{t}", [(xcur, xname, C)], C, True)

            def em_ap(v, xcur=xcur, comb=comb, nmc=nmc):
                return [v.tensor_scalar(shb[:], xcur, comb, nmc, ALU.mult, ALU.add)]
            P.add("V", em_ap, reads=[xname, sn1], writes=["shbb"], tag=f"b{b}ap{t}")
            if b == 0 and t < 2:
                dbg(f"xn2_{t}", lambda: shb[:], [128, C], BF16, ["shbb"])

            pt_, ptn = pt_next()

            def em_tr(te, pt_=pt_):
                return [te.transpose(pt_[:, kc * 128:(kc + 1) * 128],
                                     shb[:, kc * 128:(kc + 1) * 128], idm[:])
                        for kc in range(8)]
            P.add("T", em_tr, reads=["shbb", "idm"], writes=[ptn], tag=f"b{b}tr{t}")

            def em_te(v, pt_=pt_):
                return [v.tensor_copy(aT[:], pt_[:])]
            P.add("V", em_te, reads=[ptn], writes=["aTb"], tag=f"b{b}te{t}")

        partA(0)
        for t in range(NT):
            xcur = xtile[:, (t % 2) * C:(t % 2) * C + C]
            xname = f"xt{t % 2}"

            for halves, nms, nc0 in (((PA[:, 0:512], PA[:, 512:1024]), ["PA0", "PA1"], 0),
                                     ((PB[:, 0:512], PB[:, 512:1024]), ["PB0", "PB1"], 2),
                                     ((PC[:, 0:512], PD[:, 0:512]), ["PC", "PD"], 4)):
                def em_mm(te, halves=halves, nc0=nc0):
                    r = []
                    for half in range(2):
                        ncx = nc0 + half
                        for kc in range(8):
                            r.append(te.matmul(
                                halves[half],
                                aT[:, kc * 128:(kc + 1) * 128],
                                WQ[:, kc * 3 * C + ncx * 512:kc * 3 * C + ncx * 512 + 512],
                                start=(kc == 0), stop=(kc == 7)))
                    return r
                P.add("T", em_mm, reads=["aTb", "WQ"], writes=nms,
                      tag=f"b{b}mm{t}{nms[0]}")

            for src, snm, dstb, dnm in ((PA, "PA", qnb, "qnb"), (PB, "PB", knb, "knb")):
                def em_rp(v, src=src, t=t):
                    ap3 = bass.AP(src, 0, [[1024, 128], [64, 16], [1, 64]])
                    rot = bass.AP(src, 32, [[1024, 128], [64, 16], [-32, 2], [1, 32]])
                    c_ap = bass.AP(tC2, t * 64, [[NT * D, 128], [0, 16], [1, 64]])
                    s_ap = bass.AP(tS2, t * 64, [[NT * D, 128], [0, 16], [32, 2], [1, 32]])
                    r1_ = bass.AP(rsc1, 0, [[C, 128], [64, 16], [1, 64]])
                    r2_ = bass.AP(rsc2, 0, [[C, 128], [64, 16], [32, 2], [1, 32]])
                    return [v.tensor_tensor(r1_, ap3, c_ap, ALU.mult),
                            v.tensor_tensor(r2_, rot, s_ap, ALU.mult)]
                P.add("V", em_rp, reads=[snm + "0", snm + "1", "tC2", "tS2"],
                      writes=["rs1", "rs2"], tag=f"b{b}rp{t}{dnm}")

                def em_rp2(v, dstb=dstb):
                    return [v.tensor_add(dstb[:], rsc1[:], rsc2[:])]
                P.add("V", em_rp2, reads=["rs1", "rs2"], writes=[dnm],
                      tag=f"b{b}rpa{t}{dnm}")
                if b == 0 and t < 2:
                    dbg(f"{dnm}_{t}", (lambda dstb=dstb: dstb[:]), [128, C], BF16, [dnm])

            def em_ve(s, t=t):
                return [s.copy(vroll[:, (t % 4) * C:(t % 4) * C + 512], PC[:]),
                        s.copy(vroll[:, (t % 4) * C + 512:(t % 4) * C + C], PD[:])]
            P.add("S", em_ve, reads=["PC", "PD"], writes=[f"vr{t % 4}"],
                  tag=f"b{b}ve{t}")
            if b == 0 and t < 2:
                dbg(f"v_{t}", (lambda t=t: vroll[:, (t % 4) * C:(t % 4) * C + C]),
                    [128, C], BF16, [f"vr{t % 4}"])

            pt_, ptn = pt_next()

            def em_qt(te, pt_=pt_):
                return [te.transpose(
                    pt_[(h % 2) * 64:(h % 2) * 64 + 64, (h // 2) * 128:(h // 2) * 128 + 128],
                    qnb[:, h * 64:h * 64 + 64], idm[:]) for h in range(H_O)]
            P.add("T", em_qt, reads=["qnb", "idm"], writes=[ptn], tag=f"b{b}qt{t}")

            def em_qe(v, pt_=pt_):
                d_e = bass.AP(qz, 0, [[H_O * 128, 64], [256, 8], [1, 128]])
                s_e = bass.AP(pt_, 0, [[1024, 64], [128, 8], [1, 128]])
                d_o = bass.AP(qz, 64 * (H_O * 128) + 128, [[H_O * 128, 64], [256, 8], [1, 128]])
                s_o = bass.AP(pt_, 64 * 1024, [[1024, 64], [128, 8], [1, 128]])
                return [v.tensor_copy(d_e, s_e), v.tensor_copy(d_o, s_o)]
            P.add("V", em_qe, reads=[ptn], writes=["qzD"], tag=f"b{b}qe{t}")
            if b == 0 and t < 2:
                dbg(f"qz_{t}", lambda: qz[:], [128, H_O * 128], BF16, ["qzD"])

            pt_, ptn = pt_next()

            def em_kt(te, pt_=pt_):
                return [te.transpose(pt_[:, pc * 128:(pc + 1) * 128],
                                     knb[:, pc * 128:(pc + 1) * 128], idm[:])
                        for pc in range(8)]
            P.add("T", em_kt, reads=["knb", "idm"], writes=[ptn], tag=f"b{b}kt{t}")

            def em_ke(v, t=t, pt_=pt_):
                ncp = 2 if (t % 4) < 2 else 1
                d = bass.AP(kroll, (t % 4) * 128,
                            [[8 * 6 * 128, 128], [6 * 128, 8], [4 * 128, ncp], [1, 128]])
                s_ = bass.AP(pt_, 0, [[1024, 128], [128, 8], [0, ncp], [1, 128]])
                return [v.tensor_copy(d, s_)]
            P.add("V", em_ke, reads=[ptn], writes=[f"kr{t % 4}"], tag=f"b{b}ke{t}")
            if b == 0 and t < 2:
                dbg(f"kr_{t}", (lambda t=t: bass.AP(kroll, (t % 4) * 128,
                    [[8 * 4 * 128, 128], [4 * 128, 8], [1, 128]])),
                    [128, 8 * 128], BF16, [f"kr{t % 4}"])

            attention_tile(f"b{b}a", t, H_O, KB_O, KW_O, kroll, "kr", vroll, "vr",
                           4, C, mko, "mko",
                           (lambda h: PC[:, (h % 8) * 64:(h % 8) * 64 + 64] if h < 8
                            else PD[:, (h - 8) * 64:(h - 8) * 64 + 64]),
                           ["PC", "PD"], 0.125, None,
                           mid_hook=((lambda t=t: partA(t + 1)) if t + 1 < NT else None))

            def em_oe(v):
                rz0 = bass.AP(rzc, 0, [[16, 128], [1, 8], [0, 64]])
                rz1 = bass.AP(rzc, 8, [[16, 128], [1, 8], [0, 64]])
                p3a = bass.AP(PC, 0, [[512, 128], [64, 8], [1, 64]])
                p3b = bass.AP(PD, 0, [[512, 128], [64, 8], [1, 64]])
                o3a = bass.AP(ob, 0, [[C, 128], [64, 8], [1, 64]])
                o3b = bass.AP(ob, 512, [[C, 128], [64, 8], [1, 64]])
                return [v.tensor_tensor(o3a, p3a, rz0, ALU.mult),
                        v.tensor_tensor(o3b, p3b, rz1, ALU.mult)]
            P.add("V", em_oe, reads=["PC", "PD", "rzc"], writes=["obb"],
                  tag=f"b{b}oe{t}")
            if b == 0 and t < 3:
                dbg(f"ob_{t}", lambda: ob[:], [128, C], BF16, ["obb"])

            pt_, ptn = pt_next()

            def em_ot(te, pt_=pt_):
                return [te.transpose(pt_[:, kc * 128:(kc + 1) * 128],
                                     ob[:, kc * 128:(kc + 1) * 128], idm[:])
                        for kc in range(8)]
            P.add("T", em_ot, reads=["obb", "idm"], writes=[ptn], tag=f"b{b}ot{t}")

            def em_ote(v, pt_=pt_):
                return [v.tensor_copy(oT[:], pt_[:])]
            P.add("V", em_ote, reads=[ptn], writes=["oTb"], tag=f"b{b}ote{t}")

            def em_wo(te):
                r = []
                for nh in range(2):
                    for kc in range(8):
                        r.append(te.matmul(
                            PB[:, nh * 512:nh * 512 + 512],
                            oT[:, kc * 128:(kc + 1) * 128],
                            WO[:, kc * C + nh * 512:kc * C + nh * 512 + 512],
                            start=(kc == 0), stop=(kc == 7)))
                return r
            P.add("T", em_wo, reads=["oTb", "WO"], writes=["PB0", "PB1"],
                  tag=f"b{b}wo{t}")

            def em_x1(v, xcur=xcur):
                return [v.tensor_add(x1sb[:, 0:512], PB[:, 0:512], xcur[:, 0:512]),
                        v.tensor_add(x1sb[:, 512:1024], PB[:, 512:1024],
                                     xcur[:, 512:1024])]
            P.add("V", em_x1, reads=["PB0", "PB1", xname], writes=["x1s"],
                  tag=f"b{b}x1{t}")
            if b == 0 and t < 3:
                dbg(f"x1_{t}", lambda: x1sb[:], [128, C], F32, ["x1s"])

            def em_pk(q, t=t):
                return [q.dma_start(x1p_d[t * 128:(t + 1) * 128, :], x1sb[:])]
            P.add("Sy", em_pk, reads=["x1s"], writes=[f"x1p{t}"], ndma=1,
                  tag=f"b{b}pk{t}")

            comb2, nmc2, _, sn2 = ln_stats(f"b{b}h{t}", [(x1sb[:], "x1s", C)], C, False)

            def em_h0(v, t=t, comb2=comb2, nmc2=nmc2):
                return [v.tensor_scalar(xf[:, t * C:(t + 1) * C], x1sb[:], comb2, nmc2,
                                        ALU.mult, ALU.add)]
            P.add("V", em_h0, reads=["x1s", sn2], writes=[f"xf{t}"],
                  tag=f"b{b}h0{t}")
            if b == 0 and t < 2:
                dbg(f"h0_{t}", (lambda t=t: xf[:, t * C:(t + 1) * C]), [128, C], F32,
                    [f"xf{t}"])

        P.barrier()
        st_.close()

    def stageD_layer(b, l):
        st_ = ExitStack()
        FQ = mk(st_, f"FQ{b}{l}", [128, 8 * 3 * CF], BF16)
        FW = mk(st_, f"FW{b}{l}", [128, 4 * C], BF16)
        mkf = mk(st_, f"mkf{b}{l}", [128, KB_F * KW_F], BF16)
        xfTr = mk(st_, f"xfTr{b}{l}", [128, 8 * 258], BF16)
        shb2 = mk(st_, f"shb2{b}{l}", [128, C], BF16)
        qnF = mk(st_, f"qnF{b}{l}", [128, NT * CF], BF16)
        kTF = mk(st_, f"kTF{b}{l}", [128, 4 * NT * 128], BF16)
        vF = mk(st_, f"vF{b}{l}", [128, NT * CF], BF16)
        knbf = mk(st_, f"knf{b}{l}", [128, 2 * CF], BF16)
        rsc3 = rsc4
        qsB = mk(st_, f"qsB{b}{l}", [128, D], F32)
        ksB = mk(st_, f"ksB{b}{l}", [128, D], F32)
        cqF = mk(st_, f"cqF{b}{l}", [128, NT * 8], F32)
        bT = {}
        for qk in ("q", "k"):
            for cs in ("c", "s"):
                bT[qk, cs] = mk(st_, f"b{qk}{cs}{b}{l}", [128, NT * D], F32)

        def em_w(q):
            return [_ld_chunks(q, FQ, fqkv_d[l * C:(l + 1) * C, :], 8, 3 * CF),
                    _ld_chunks(q, FW, fwo_d[l * CF:(l + 1) * CF, :], 4, C)]
        P.add("G", em_w, writes=["FQ", "FW"], ndma=2, tag=f"d{b}{l}w")

        def em_tl(q):
            r = [q.dma_start(qsB[:], qsb_d[l * 128:(l + 1) * 128, :]),
                 q.dma_start(ksB[:], ksb_d[l * 128:(l + 1) * 128, :]),
                 _ld_chunks(q, mkf, mkf_d, KB_F, KW_F)]
            for qk in ("q", "k"):
                for cs in ("c", "s"):
                    r.append(q.dma_start(bT[qk, cs][:], tcx_d[qk, cs]))
            return r
        P.add("Sy", em_tl, writes=["qsB", "ksB", "bT", "mkf"], ndma=7,
              tag=f"d{b}{l}tl")

        def em_xz(v):
            zp = bass.AP(xfTr, 0, [[8 * 258, 128], [258, 8], [129, 2], [1, 1]])
            return [v.memset(zp, 0.0)]
        P.add("V", em_xz, writes=["xfTs0", "xfTs1"], tag=f"d{b}{l}xz")

        def em_bt(v):
            r = []
            for qk, scl in (("q", qsB), ("k", ksB)):
                tC = bT[qk, "c"]; tS = bT[qk, "s"]
                c3 = bass.AP(tC, 0, [[NT * D, 128], [D, NT], [1, D]])
                sc3 = bass.AP(scl, 0, [[D, 128], [0, NT], [1, D]])
                r.append(v.tensor_tensor(c3, c3, sc3, ALU.mult))
                s4 = bass.AP(tS, 0, [[NT * D, 128], [D, NT], [32, 2], [1, 32]])
                scr = bass.AP(scl, 32, [[D, 128], [0, NT], [-32, 2], [1, 32]])
                r.append(v.tensor_tensor(s4, s4, scr, ALU.mult))
            return r
        P.add("V", em_bt, reads=["bT", "qsB", "ksB"], writes=["bT"], tag=f"d{b}{l}bt")

        def ph1a(t):
            sl = t % 2
            shbc = shb if sl == 0 else shb2
            shbn = "shbb" if sl == 0 else "shbb2"
            xfn = f"xfTs{sl}"

            def em_xc(v, t=t, shbc=shbc):
                return [v.tensor_copy(shbc[:], xf[:, t * C:(t + 1) * C])]
            P.add("V", em_xc, reads=[f"xf{t}"], writes=[shbn], tag=f"d{b}{l}xc{t}")

            pt_, ptn = pt_next()

            def em_xt(te, shbc=shbc, pt_=pt_):
                return [te.transpose(pt_[:, kc * 128:(kc + 1) * 128],
                                     shbc[:, kc * 128:(kc + 1) * 128], idm[:])
                        for kc in range(8)]
            P.add("T", em_xt, reads=[shbn, "idm"], writes=[ptn], tag=f"d{b}{l}xt{t}")

            def em_xe(v, sl=sl, pt_=pt_):
                d = bass.AP(xfTr, sl * 129 + 1, [[8 * 258, 128], [258, 8], [1, 128]])
                s_ = bass.AP(pt_, 0, [[1024, 128], [128, 8], [1, 128]])
                d2 = bass.AP(xfTr, (1 - sl) * 129, [[8 * 258, 128], [258, 8], [1, 1]])
                s2 = bass.AP(pt_, 127, [[1024, 128], [128, 8], [1, 1]])
                return [v.tensor_copy(d, s_), v.tensor_copy(d2, s2)]
            P.add("V", em_xe, reads=[ptn], writes=[xfn, f"xfTs{1 - sl}"],
                  tag=f"d{b}{l}xe{t}")

        def ph1b(t):
            sl = t % 2
            xfn = f"xfTs{sl}"

            def mmgrp(ncx, pt_t, pt_o, pname):
                def em(te, ncx=ncx, pt_t=pt_t, pt_o=pt_o, sl=sl):
                    r = []
                    for kc in range(8):
                        off = sl * 129 + (1 if kc < 4 else 0)
                        r.append(te.matmul(
                            pt_t[:, pt_o:pt_o + 512],
                            xfTr[:, kc * 258 + off:kc * 258 + off + 128],
                            FQ[:, kc * 3 * CF + ncx * 512:kc * 3 * CF + ncx * 512 + 512],
                            start=(kc == 0), stop=(kc == 7)))
                    return r
                P.add("T", em, reads=[xfn, "FQ"], writes=[pname],
                      tag=f"d{b}{l}mm{ncx}{t}")
            mmgrp(1, PA, 512, "PA1")
            mmgrp(0, PA, 0, "PA0")
            mmgrp(2, PB, 0, "PB0")
            qp_t, qp_o, qp_n, qp_w = PA, 0, "PA0", 1024
            kp_t, kp_o, kp_n, kp_w = PA, 512, "PA1", 1024
            vp_t, vp_o, vp_n, vp_w = PB, 0, "PB0", 1024
            sqs = rsc2 if sl == 0 else rsc4
            sqn_ = "rs2" if sl == 0 else "rs4"
            po8 = sl * 8

            def em_sqk(s, kp_t=kp_t, kp_o=kp_o, sqs=sqs):
                return [s.activation(sqs[:, 512:1024], kp_t[:, kp_o:kp_o + 512],
                                     AF.Square)]
            P.add("S", em_sqk, reads=[kp_n], writes=[sqn_], tag=f"d{b}{l}sqk{t}")

            def em_sqq(s, qp_t=qp_t, qp_o=qp_o, sqs=sqs):
                return [s.activation(sqs[:, 0:512], qp_t[:, qp_o:qp_o + 512],
                                     AF.Square)]
            P.add("S", em_sqq, reads=[qp_n], writes=[sqn_], tag=f"d{b}{l}sqq{t}")

            def em_rqk(v, sqs=sqs, po8=po8):
                k3 = bass.AP(sqs, 512, [[C, 128], [64, 8], [1, 64]])
                return [v.reduce_sum(ssk8[:, po8:po8 + 8], k3, axis=AX.X)]
            P.add("V", em_rqk, reads=[sqn_], writes=[f"ssk{sl}"],
                  tag=f"d{b}{l}rqk{t}")

            def em_rqq(v, sqs=sqs, po8=po8):
                a3 = bass.AP(sqs, 0, [[C, 128], [64, 8], [1, 64]])
                return [v.reduce_sum(ssq8[:, po8:po8 + 8], a3, axis=AX.X)]
            P.add("V", em_rqq, reads=[sqn_], writes=[f"ssq{sl}"],
                  tag=f"d{b}{l}rqq{t}")

            def em_rpq(v, t=t, qp_t=qp_t, qp_o=qp_o, qp_w=qp_w):
                ap3 = bass.AP(qp_t, qp_o, [[qp_w, 128], [64, 8], [1, 64]])
                rot = bass.AP(qp_t, qp_o + 32, [[qp_w, 128], [64, 8], [-32, 2], [1, 32]])
                c_ap = bass.AP(bT["q", "c"], t * 64, [[NT * D, 128], [0, 8], [1, 64]])
                s_ap = bass.AP(bT["q", "s"], t * 64, [[NT * D, 128], [0, 8], [32, 2], [1, 32]])
                r1_ = bass.AP(rsc1, 0, [[C, 128], [64, 8], [1, 64]])
                r2_ = bass.AP(rsc1, 512, [[C, 128], [64, 8], [32, 2], [1, 32]])
                return [v.tensor_tensor(r1_, ap3, c_ap, ALU.mult),
                        v.tensor_tensor(r2_, rot, s_ap, ALU.mult)]
            P.add("V", em_rpq, reads=[qp_n, "bT"], writes=["rs1"],
                  tag=f"d{b}{l}rpq{t}")

            def em_rpq2(v, t=t):
                return [v.tensor_add(qnF[:, t * CF:(t + 1) * CF], rsc1[:, 0:512],
                                     rsc1[:, 512:1024])]
            P.add("V", em_rpq2, reads=["rs1"], writes=["qnF"], tag=f"d{b}{l}rpq2{t}")

            def em_snk(s, po8=po8):
                return [s.activation(sq8b[:, po8:po8 + 8], ssk8[:, po8:po8 + 8],
                                     AF.Ln)]
            P.add("S", em_snk, reads=[f"ssk{sl}"], writes=[f"sq8k{sl}"],
                  tag=f"d{b}{l}snk{t}")

            def em_snq(s, po8=po8):
                return [s.activation(sq8a[:, po8:po8 + 8], ssq8[:, po8:po8 + 8],
                                     AF.Ln)]
            P.add("S", em_snq, reads=[f"ssq{sl}"], writes=[f"sq8q{sl}"],
                  tag=f"d{b}{l}snq{t}")

            def em_rnk(s, po8=po8):
                return [s.activation(rk8[:, po8:po8 + 8], sq8b[:, po8:po8 + 8],
                                     AF.Exp, scale=-0.5)]
            P.add("S", em_rnk, reads=[f"sq8k{sl}"], writes=[f"rk{sl}"],
                  tag=f"d{b}{l}rnk{t}")

            def em_rnq(s, t=t, po8=po8):
                return [s.activation(cqF[:, t * 8:t * 8 + 8], sq8a[:, po8:po8 + 8],
                                     AF.Exp, scale=-0.5)]
            P.add("S", em_rnq, reads=[f"sq8q{sl}"], writes=["cqF"],
                  tag=f"d{b}{l}rnq{t}")

            def em_rn2k(v, po8=po8):
                return [v.tensor_scalar_min(rk8[:, po8:po8 + 8],
                                            rk8[:, po8:po8 + 8], 1.0e12)]
            P.add("V", em_rn2k, reads=[f"rk{sl}"], writes=[f"rk{sl}"],
                  tag=f"d{b}{l}rn2k{t}")

            def em_rn2q(v, t=t):
                return [v.tensor_scalar(cqF[:, t * 8:t * 8 + 8],
                                        cqF[:, t * 8:t * 8 + 8], 1.0e12, 8.0,
                                        ALU.min, ALU.mult)]
            P.add("V", em_rn2q, reads=["cqF"], writes=["cqF"],
                  tag=f"d{b}{l}rn2q{t}")


            def em_rpk(v, kp_t=kp_t, kp_o=kp_o, kp_w=kp_w, po8=po8):
                k3 = bass.AP(kp_t, kp_o, [[kp_w, 128], [64, 8], [1, 64]])
                rk_ap = bass.AP(rk8, po8, [[16, 128], [1, 8], [0, 64]])
                k1w = bass.AP(rsc3, 0, [[C, 128], [64, 8], [1, 64]])
                return [v.tensor_tensor(k1w, k3, rk_ap, ALU.mult)]
            P.add("V", em_rpk, reads=[kp_n, f"rk{sl}"], writes=["rs4"],
                  tag=f"d{b}{l}rpk{t}")

            def em_rpk2(v, t=t):
                k1r = bass.AP(rsc3, 0, [[C, 128], [64, 8], [1, 64]])
                rot = bass.AP(rsc3, 32, [[C, 128], [64, 8], [-32, 2], [1, 32]])
                c_ap = bass.AP(bT["k", "c"], t * 64, [[NT * D, 128], [0, 8], [1, 64]])
                s_ap = bass.AP(bT["k", "s"], t * 64, [[NT * D, 128], [0, 8], [32, 2], [1, 32]])
                r1_ = bass.AP(rsc1, 0, [[C, 128], [64, 8], [1, 64]])
                r2_ = bass.AP(rsc1, 512, [[C, 128], [64, 8], [32, 2], [1, 32]])
                return [v.tensor_tensor(r1_, k1r, c_ap, ALU.mult),
                        v.tensor_tensor(r2_, rot, s_ap, ALU.mult)]
            P.add("V", em_rpk2, reads=["rs4", "bT"], writes=["rs1"],
                  tag=f"d{b}{l}rpk2{t}")

            kn_c = knbf[:, sl * CF:sl * CF + CF]
            knn = f"knf{sl}"

            def em_rpk3(v, kn_c=kn_c):
                return [v.tensor_add(kn_c, rsc1[:, 0:512], rsc1[:, 512:1024])]
            P.add("V", em_rpk3, reads=["rs1"], writes=[knn], tag=f"d{b}{l}rpk3{t}")

            def em_ve(s, t=t, vp_t=vp_t, vp_o=vp_o):
                return [s.copy(vF[:, t * CF:(t + 1) * CF], vp_t[:, vp_o:vp_o + 512])]
            P.add("S", em_ve, reads=[vp_n], writes=["vF"], tag=f"d{b}{l}ve{t}")

        def ph1c(t):
            sl = t % 2
            knn = f"knf{sl}"
            pt2_, ptn2 = pt_next()

            def em_kt(te, sl=sl, pt2_=pt2_):
                return [te.transpose(pt2_[:, pc * 128:(pc + 1) * 128],
                                     bass.AP(knbf, sl * CF + pc * 128,
                                             [[2 * CF, 128], [1, 128]]), idm[:])
                        for pc in range(4)]
            P.add("T", em_kt, reads=[knn, "idm"], writes=[ptn2], tag=f"d{b}{l}kt{t}")

            def em_ke(v, t=t, pt2_=pt2_):
                d = bass.AP(kTF, t * 128, [[4 * NT * 128, 128], [NT * 128, 4], [1, 128]])
                s_ = bass.AP(pt2_, 0, [[1024, 128], [128, 4], [1, 128]])
                return [v.tensor_copy(d, s_)]
            P.add("V", em_ke, reads=[ptn2], writes=["kTF"], tag=f"d{b}{l}ke{t}")

        def qprep(t):
            qoff = (t % 2) * 1024
            qn_ = f"qzD{t % 2}"
            pt_, ptn = pt_next()

            def em_qt(te, t=t, pt_=pt_):
                return [te.transpose(
                    pt_[(h % 2) * 64:(h % 2) * 64 + 64, (h // 2) * 128:(h // 2) * 128 + 128],
                    qnF[:, t * CF + h * 64:t * CF + h * 64 + 64], idm[:])
                    for h in range(H_F)]
            P.add("T", em_qt, reads=["qnF", "idm"], writes=[ptn], tag=f"d{b}{l}qt{t}")

            def em_qe(v, pt_=pt_, qoff=qoff):
                d_e = bass.AP(qz, qoff, [[H_O * 128, 64], [256, 4], [1, 128]])
                s_e = bass.AP(pt_, 0, [[1024, 64], [128, 4], [1, 128]])
                d_o = bass.AP(qz, 64 * (H_O * 128) + qoff + 128,
                              [[H_O * 128, 64], [256, 4], [1, 128]])
                s_o = bass.AP(pt_, 64 * 1024, [[1024, 64], [128, 4], [1, 128]])
                return [v.tensor_copy(d_e, s_e), v.tensor_copy(d_o, s_o)]
            P.add("V", em_qe, reads=[ptn], writes=[qn_], tag=f"d{b}{l}qe{t}")

        ph1a(0)
        ph1b(0)
        ph1c(0)
        ph1a(1)
        ph1b(1)
        qprep(0)
        for t in range(NT):
            hooks = {2: [], 5: []}
            if t + 1 < NT:
                hooks[2].append(lambda t=t: qprep(t + 1))
                hooks[2].append(lambda t=t: ph1c(t + 1))
            if t + 2 < NT:
                hooks[2].append(lambda t=t: ph1a(t + 2))
                hooks[5].append(lambda t=t: ph1b(t + 2))
            attention_tile(f"d{b}{l}a", t, H_F, KB_F, KW_F, kTF, None, vF, None,
                           NT, CF, mkf, "mkf",
                           (lambda h: PD[:, h * 64:h * 64 + 64]), ["PD"], 1.0,
                           (lambda h, t=t: cqF[:, t * 8 + h:t * 8 + h + 1]),
                           full_kv=True, hooks=hooks, cq_name="cqF",
                           qz_off=(t % 2) * 1024, qz_name=f"qzD{t % 2}")

            def em_oe(v):
                rz0 = bass.AP(rzc, 0, [[16, 128], [1, 8], [0, 64]])
                p3 = bass.AP(PD, 0, [[512, 128], [64, 8], [1, 64]])
                o3 = bass.AP(ob, 0, [[C, 128], [64, 8], [1, 64]])
                return [v.tensor_tensor(o3, p3, rz0, ALU.mult)]
            P.add("V", em_oe, reads=["PD", "rzc"], writes=["obb"],
                  tag=f"d{b}{l}oe{t}")

            pt2_, ptn2 = pt_next()

            def em_ot(te, pt2_=pt2_):
                return [te.transpose(pt2_[:, kc * 128:(kc + 1) * 128],
                                     ob[:, kc * 128:(kc + 1) * 128], idm[:])
                        for kc in range(4)]
            P.add("T", em_ot, reads=["obb", "idm"], writes=[ptn2], tag=f"d{b}{l}ot{t}")

            def em_ote(v, pt2_=pt2_):
                return [v.tensor_copy(oT[:, 0:512], pt2_[:, 0:512])]
            P.add("V", em_ote, reads=[ptn2], writes=["oTb"], tag=f"d{b}{l}ote{t}")

            def em_wo(te):
                r = []
                for nh, dst in ((0, PC[:, 0:512]), (1, PD[:, 0:512])):
                    for kc in range(4):
                        r.append(te.matmul(
                            dst, oT[:, kc * 128:(kc + 1) * 128],
                            FW[:, kc * C + nh * 512:kc * C + nh * 512 + 512],
                            start=(kc == 0), stop=(kc == 3)))
                return r
            P.add("T", em_wo, reads=["oTb", "FW"], writes=["PC", "PD"],
                  tag=f"d{b}{l}wo{t}")

            def em_xf(v, t=t):
                return [v.tensor_add(xf[:, t * C:t * C + 512], xf[:, t * C:t * C + 512],
                                     PC[:]),
                        v.tensor_add(xf[:, t * C + 512:(t + 1) * C],
                                     xf[:, t * C + 512:(t + 1) * C], PD[:])]
            P.add("V", em_xf, reads=["PC", "PD", f"xf{t}"], writes=[f"xf{t}"],
                  tag=f"d{b}{l}xf{t}")

        P.barrier()
        st_.close()

        st2 = ExitStack()
        shn2T = mk(st2, f"s2T{b}{l}", [128, 8 * T], BF16)
        b1s = mk(st2, f"b1s{b}{l}", [128, 2 * NPAIR], F32)
        b2b = mk(st2, f"b2b{b}{l}", [1, C], BF16)
        stp = ExitStack()
        lgB = mk(stp, f"lgB{b}{l}", [128, C], F32)
        lbB = mk(stp, f"lbB{b}{l}", [128, C], F32)
        shbP = mk(stp, f"shbP{b}{l}", [128, C], BF16)

        def em_ls(q):
            return [q.dma_start(lgB[:], lgb_d[l * 128:(l + 1) * 128, :]),
                    q.dma_start(lbB[:], lbb_d[l * 128:(l + 1) * 128, :]),
                    q.dma_start(b1s[:], fb1_d[l * 128:(l + 1) * 128, :])]
        P.add("Sy", em_ls, writes=["lgB", "lbB", "b1s"], ndma=3, tag=f"f{b}{l}ls")

        def em_lb2(q):
            return [q.dma_start(b2b[:], fb2_d[l:l + 1, :])]
        P.add("G", em_lb2, writes=["b2b"], ndma=1, tag=f"f{b}{l}b2")

        for t in range(NT):
            sp = PD if t % 2 == 0 else PC
            spn = "PD" if t % 2 == 0 else "PC"

            def em_sh(te, t=t, sp=sp):
                r = [te.matmul(sp[:, 0:512], shmF[:],
                               xf[:, t * C + 512:(t + 1) * C],
                               start=True, stop=(t == 0), skip_group_check=True)]
                if t > 0:
                    r.append(te.matmul(sp[:, 0:512], e127F[:],
                                       xf[:, (t - 1) * C + 512:t * C],
                                       start=False, stop=True,
                                       skip_group_check=True))
                return r
            P.add("T", em_sh,
                  reads=["shmF", "e127F", f"xf{t}"] + ([f"xf{t - 1}"] if t else [])
                  + ([f"s2T{t - 2}"] if t >= 2 else []),
                  writes=[spn], tag=f"f{b}{l}sh{t}")

            comb3, nmc3, mean3, sn3 = ln_stats(
                f"f{b}{l}n{t}",
                [(xf[:, t * C:t * C + 512], f"xf{t}", 512), (sp[:, 0:512], spn, 512)],
                C, False)

            def em_ap(v, t=t, mean3=mean3, sp=sp):
                r = []
                for half, srcp in enumerate((xf[:, t * C:t * C + 512], sp[:, 0:512])):
                    o = half * 512
                    r.append(v.scalar_tensor_tensor(rsc1[:, o:o + 512], srcp, mean3,
                                                    lgB[:, o:o + 512],
                                                    ALU.subtract, ALU.mult))
                return r
            P.add("V", em_ap, reads=[f"xf{t}", spn, sn3, "lgB"],
                  writes=["rs1"], tag=f"f{b}{l}ap{t}")

            shbc2 = shb if t % 2 == 0 else shbP
            shbn2 = "shbb" if t % 2 == 0 else "shbP"

            def em_ap2(v, comb3=comb3, shbc2=shbc2):
                return [v.scalar_tensor_tensor(shbc2[:, o:o + 512], rsc1[:, o:o + 512],
                                               comb3, lbB[:, o:o + 512],
                                               ALU.mult, ALU.add)
                        for o in (0, 512)]
            P.add("V", em_ap2, reads=["rs1", sn3, "lbB"], writes=[shbn2],
                  tag=f"f{b}{l}ap2{t}")
            if b == 0 and l == 0 and t < 2:
                dbg(f"fsh_{t}", lambda: shb[:], [128, C], BF16, ["shbb"])

            pt_, ptn = pt_next()

            def em_tr(te, pt_=pt_, shbc2=shbc2):
                return [te.transpose(pt_[:, kc * 128:(kc + 1) * 128],
                                     shbc2[:, kc * 128:(kc + 1) * 128], idm[:])
                        for kc in range(8)]
            P.add("T", em_tr, reads=[shbn2, "idm"], writes=[ptn], tag=f"f{b}{l}tr{t}")

            def em_te2(v, t=t, pt_=pt_):
                d = bass.AP(shn2T, t * 128, [[8 * T, 128], [T, 8], [1, 128]])
                s_ = bass.AP(pt_, 0, [[1024, 128], [128, 8], [1, 128]])
                return [v.tensor_copy(d, s_)]
            P.add("V", em_te2, reads=[ptn], writes=[f"s2T{t}"], tag=f"f{b}{l}te{t}")

        P.barrier()
        stp.close()
        x1sb2 = mk(st2, f"x1r{b}{l}", [128, 2 * C], F32)
        nspl = [max(len(s) for i, s in enumerate(SPLITS) if i % 2 == j)
                for j in range(2)]
        W1d = [mk(st2, f"W1_{b}{l}{j}", [128, 8 * nspl[j] * 256], BF16)
               for j in range(2)]
        W2d = [mk(st2, f"W2_{b}{l}{j}", [128, nspl[j] * C], BF16) for j in range(2)]
        acd = [mk(st2, f"ac{b}{l}{j}", [128, nspl[j] * 512], BF16) for j in range(2)]
        for ith, split in enumerate(SPLITS):
            npp = len(split)
            p0 = split[0]
            par3 = ith % 2
            W1t, W2t, actb = W1d[par3], W2d[par3], acd[par3]
            w1n, w2n, acn = f"W1t{par3}", f"W2t{par3}", f"actb{par3}"

            def em_w1(q, p0=p0, npp=npp, W1t=W1t, W2t=W2t):
                return [_ld_chunks(q, W1t, fw1_d[l * C:(l + 1) * C,
                                                        p0 * 256:(p0 + npp) * 256],
                                           8, npp * 256),
                        _ld_chunks(q, W2t, fw2_d[l * W2R + p0 * 128:
                                                 l * W2R + (p0 + npp) * 128, :],
                                   npp, C)]
            P.add("G", em_w1, writes=[w1n, w2n], ndma=2, tag=f"f{b}{l}w{ith}")

            for sbk in range(4):
                tok0 = sbk * 512
                for ip, pr in enumerate(split):
                    pp = PA if ip % 2 == 0 else PB
                    ppn = ["PA0", "PA1"] if ip % 2 == 0 else ["PB0", "PB1"]
                    ggs = rsc2[:, 0:512] if ip % 2 == 0 else rsc2[:, 512:1024]
                    ggn = "rs2a" if ip % 2 == 0 else "rs2b"

                    def em_m1(te, ip=ip, tok0=tok0, W1t=W1t, npp=npp, pp=pp):
                        r = []
                        for half in range(2):
                            for kc in range(8):
                                r.append(te.matmul(
                                    pp[:, half * 512:half * 512 + 512],
                                    W1t[:, kc * npp * 256 + ip * 256 + half * 128:
                                        kc * npp * 256 + ip * 256 + half * 128 + 128],
                                    bass.AP(shn2T, kc * T + tok0, [[8 * T, 128], [1, 512]]),
                                    start=(kc == 0), stop=(kc == 7)))
                        return r
                    P.add("T", em_m1,
                          reads=[w1n] + [f"s2T{sbk * 4 + j}" for j in range(4)],
                          writes=ppn, tag=f"f{b}{l}m1_{ith}_{sbk}_{ip}")

                    def em_gg2(s, pr=pr, pp=pp, ggs=ggs):
                        return [s.activation(ggs, pp[:, 512:1024], AF.Gelu,
                                             bias=b1s[:, 2 * pr + 1:2 * pr + 2])]
                    P.add("S", em_gg2, reads=[ppn[1], "b1s"], writes=[ggn],
                          tag=f"f{b}{l}gg_{ith}_{sbk}_{ip}")

                    def em_ac(v, ip=ip, pr=pr, actb=actb, pp=pp, ggs=ggs):
                        return [v.scalar_tensor_tensor(
                            actb[:, ip * 512:(ip + 1) * 512], pp[:, 0:512],
                            b1s[:, 2 * pr:2 * pr + 1], ggs,
                            ALU.add, ALU.mult)]
                    P.add("V", em_ac, reads=[ppn[0], "b1s", ggn], writes=[acn],
                          tag=f"f{b}{l}ac_{ith}_{sbk}_{ip}")
                    if b == 0 and l == 0 and ith == 0 and sbk == 0 and ip == 0:
                        dbg("fact0", (lambda actb=actb: actb[:, 0:512]), [128, 512],
                            BF16, ["actb"])

                for st in range(4):
                    t_glob = sbk * 4 + st
                    last = (ith == len(SPLITS) - 1)
                    for nh in range(2):
                        w2dst = PC[:, 0:512] if nh == 0 else PD[:, 0:512]
                        w2nm = "PC" if nh == 0 else "PD"

                        def em_m2(te, st=st, nh=nh, actb=actb, W2t=W2t, npp=npp,
                                  last=last, w2dst=w2dst):
                            r = []
                            for c in range(npp):
                                r.append(te.matmul(
                                    w2dst,
                                    actb[:, c * 512 + st * 128:c * 512 + st * 128 + 128],
                                    W2t[:, c * C + nh * 512:c * C + nh * 512 + 512],
                                    start=(c == 0), stop=(c == npp - 1 and not last),
                                    skip_group_check=True))
                            if last:
                                r.append(te.matmul(w2dst, ones1[:],
                                                   b2b[:, nh * 512:nh * 512 + 512],
                                                   start=False, stop=True,
                                                   skip_group_check=True))
                            return r
                        P.add("T", em_m2,
                              reads=[acn, w2n] + (["ones1", "b2b"] if last else []),
                              writes=[w2nm], tag=f"f{b}{l}m2_{ith}_{sbk}_{st}{nh}")

                    if not (last and l == 1):
                        def em_up(v, t_glob=t_glob):
                            return [v.tensor_add(xf[:, t_glob * C:t_glob * C + 512],
                                                 xf[:, t_glob * C:t_glob * C + 512],
                                                 PC[:]),
                                    v.tensor_add(xf[:, t_glob * C + 512:(t_glob + 1) * C],
                                                 xf[:, t_glob * C + 512:(t_glob + 1) * C],
                                                 PD[:])]
                        P.add("V", em_up, reads=["PC", "PD", f"xf{t_glob}"],
                              writes=[f"xf{t_glob}"], tag=f"f{b}{l}up{ith}_{t_glob}")
                    else:
                        xsl = (t_glob % 2) * C

                        def em_un(q, t_glob=t_glob, xsl=xsl):
                            return [q.dma_start(
                                bass.AP(x1sb2, xsl, [[2 * C, 128], [1, C]]),
                                x1p_d[t_glob * 128:(t_glob + 1) * 128, :])]
                        P.add("Sy", em_un, reads=[f"x1p{t_glob}"],
                              writes=[f"x1r{t_glob % 2}"], ndma=1,
                              tag=f"f{b}{l}un{t_glob}")

                        def em_fo(v, t_glob=t_glob, xsl=xsl):
                            return [
                                v.tensor_add(x1sb2[:, xsl:xsl + 512], PC[:],
                                             x1sb2[:, xsl:xsl + 512]),
                                v.tensor_add(x1sb2[:, xsl + 512:xsl + C], PD[:],
                                             x1sb2[:, xsl + 512:xsl + C])]
                        P.add("V", em_fo, reads=["PC", "PD", f"x1r{t_glob % 2}"],
                              writes=[f"x1r{t_glob % 2}"], tag=f"f{b}{l}fo{t_glob}")

                        def em_fo2(v, t_glob=t_glob, xsl=xsl):
                            return [v.tensor_add(x1sb2[:, xsl:xsl + 512],
                                                 x1sb2[:, xsl:xsl + 512],
                                                 xf[:, t_glob * C:t_glob * C + 512]),
                                    v.tensor_add(x1sb2[:, xsl + 512:xsl + C],
                                                 x1sb2[:, xsl + 512:xsl + C],
                                                 xf[:, t_glob * C + 512:(t_glob + 1) * C])]
                        P.add("V", em_fo2, reads=[f"x1r{t_glob % 2}", f"xf{t_glob}"],
                              writes=[f"x1r{t_glob % 2}"], tag=f"f{b}{l}fo2{t_glob}")

                        def em_od(q, t_glob=t_glob, b=b, xsl=xsl):
                            return [q.dma_start(out_d[b * T + t_glob * 128:
                                                      b * T + (t_glob + 1) * 128, :],
                                                bass.AP(x1sb2, xsl,
                                                        [[2 * C, 128], [1, C]]))]
                        P.add("Sy", em_od, reads=[f"x1r{t_glob % 2}"],
                              writes=[f"od{b}_{t_glob}"],
                              ndma=1, tag=f"f{b}{l}od{t_glob}")
        P.barrier()
        st2.close()

    for b in range(nb):
        stageB(b)
        stageD_layer(b, 0)
        stageD_layer(b, 1)

    P.finalize()

    semstack = ExitStack()
    sems = {e: semstack.enter_context(nc.semaphore(f"sem{e}"))
            for e in ("T", "V", "S", "Sy", "G")}

    with root, semstack, nc.Block() as blk:
        def run_engine(eng_name):
            def body(engine):
                floors = {}
                for t in P.tasks:
                    if t.eng != eng_name:
                        continue
                    for we, wv in sorted(t.waits.items()):
                        if floors.get(we, -1) >= wv:
                            continue
                        engine.wait_ge(sems[we], wv)
                        floors[we] = wv
                    instrs = t.emit(engine)
                    if t.flag:
                        if eng_name in DMA_ENGS:
                            assert len(instrs) * 16 == t.inc, \
                                f"{t.tag}: ndma {t.inc // 16} != {len(instrs)}"
                            for ins in instrs:
                                ins.then_inc(sems[eng_name], 16)
                        else:
                            instrs[-1].then_inc(sems[eng_name], 1)
                if eng_name == "Sy":
                    engine.wait_ge(sems["Sy"], P.final_counts.get("Sy", 0))
            return body

        blk.tensor(run_engine("T"))
        blk.vector(run_engine("V"))
        blk.scalar(run_engine("S"))
        blk.sync(run_engine("Sy"))
        blk.gpsimd(run_engine("G"))

    return nc


def kernel(L0_attn_x, L0_attn_w_qkv, L0_attn_w_out, L0_ffn_wqkv, L0_ffn_q_scale,
           L0_ffn_k_scale, L0_ffn_wo, L0_ffn_ln_g, L0_ffn_ln_b, L0_ffn_w1, L0_ffn_b1,
           L0_ffn_w2, L0_ffn_b2):
    f = np.float32
    x = np.asarray(L0_attn_x, f)
    wqt = np.ascontiguousarray(np.asarray(L0_attn_w_qkv, f).T)
    wot = np.ascontiguousarray(np.asarray(L0_attn_w_out, f).T)
    fqkv = np.ascontiguousarray(np.asarray(L0_ffn_wqkv, f).reshape(2 * C, 3 * CF))
    fwo = np.ascontiguousarray(np.asarray(L0_ffn_wo, f).reshape(2 * CF, C))
    w1 = np.asarray(L0_ffn_w1, f)
    w2 = np.asarray(L0_ffn_w2, f)
    b1 = np.asarray(L0_ffn_b1, f)
    b2v = np.ascontiguousarray(np.asarray(L0_ffn_b2, f))
    fw1 = np.ascontiguousarray(np.concatenate([permute_w1(w1[i]) for i in range(2)], axis=0))
    fw2 = np.ascontiguousarray(np.concatenate([pad_w2(w2[i]) for i in range(2)], axis=0))
    fb1 = np.ascontiguousarray(np.concatenate([permute_b1(b1[i]) for i in range(2)], axis=0))
    tile2 = lambda a: np.ascontiguousarray(
        np.concatenate([np.tile(np.asarray(a, f)[i], (128, 1)) for i in range(2)], axis=0))
    lgb = tile2(L0_ffn_ln_g)
    lbb = tile2(L0_ffn_ln_b)
    qsb = tile2(L0_ffn_q_scale)
    ksb = tile2(L0_ffn_k_scale)
    tc2, ts2v, tcxq, tsxq, tcxk, tsxk = make_tables()
    mko, mkfv = make_masks()
    ident = np.eye(128, dtype=BFDT)
    shm = np.eye(128, k=1).astype(BFDT)
    e127 = np.zeros((128, 128), BFDT)
    e127[127, 0] = 1
    ones1 = np.ones((1, 128), BFDT)

    nc = build(NB)
    shmf = np.eye(128, k=1).astype(np.float32)
    e127f = np.zeros((128, 128), np.float32)
    e127f[127, 0] = 1.0
    common = dict(shmf=shmf, e127f=e127f,
                  wqt=wqt, wot=wot, fqkv=fqkv, fwo=fwo, fw1=fw1, fw2=fw2, fb1=fb1,
                  fb2=b2v, lgb=lgb, lbb=lbb, qsb=qsb, ksb=ksb,
                  tc2=tc2, ts2=ts2v, tcxq=tcxq, tsxq=tsxq, tcxk=tcxk, tsxk=tsxk,
                  mko=mko, mkf=mkfv, ident=ident, shm=shm, e127=e127, ones1=ones1)
    in_maps = []
    for cidx in range(8):
        m = dict(common)
        m["x"] = np.ascontiguousarray(x[cidx * NB:(cidx + 1) * NB].reshape(NB * T, C))
        in_maps.append(m)
    res = run_bass_kernel_spmd(nc, in_maps, core_ids=list(range(8)))
    out = np.concatenate([np.asarray(r["out"]).reshape(NB, T, C) for r in res.results],
                         axis=0)
    return out


# version 25
# speedup vs baseline: 1.0254x; 1.0254x over previous
import numpy as np
import ml_dtypes
from contextlib import ExitStack

import concourse.bass as bass
import concourse.mybir as mybir
from concourse.bass_utils import run_bass_kernel_spmd

F32 = mybir.dt.float32
BF16 = mybir.dt.bfloat16
AF = mybir.ActivationFunctionType
ALU = mybir.AluOpType
AX = mybir.AxisListType
BFDT = ml_dtypes.bfloat16

T = 2048
C = 1024
NT = 16
NB = 4
H_O, D = 16, 64
KB_O = 3
KW_O = KB_O * 128
H_F, CF = 8, 512
KB_F = 5
KW_F = KB_F * 128
NPAIR = 22
W1C = NPAIR * 256
W2R = NPAIR * 128
SPLITS = [range(0, 5), range(5, 9), range(9, 14), range(14, 18), range(18, 22)]
NEG = -1.0e30

DMA_ENGS = ("Sy", "G")
DEBUG = False


class Task:
    __slots__ = ("eng", "emit", "reads", "writes", "inc", "deps", "flag", "cum",
                 "waits", "tag")

    def __init__(self, eng, emit, reads, writes, inc, tag):
        self.eng = eng
        self.emit = emit
        self.reads = reads
        self.writes = writes
        self.inc = inc
        self.deps = set()
        self.flag = eng in DMA_ENGS
        self.cum = 0
        self.waits = {}
        self.tag = tag


class Prog:
    def __init__(self):
        self.tasks = []
        self.last_write = {}
        self.readers = {}
        self.floor = None

    def add(self, eng, emit, reads=(), writes=(), ndma=1, tag=""):
        inc = 16 * ndma if eng in DMA_ENGS else 1
        t = Task(eng, emit, list(reads), list(writes), inc, tag)
        i = len(self.tasks)
        for b in t.reads:
            w = self.last_write.get(b)
            if w is not None:
                t.deps.add(w)
            self.readers.setdefault(b, []).append(i)
        for b in t.writes:
            w = self.last_write.get(b)
            if w is not None:
                t.deps.add(w)
            for rd in self.readers.get(b, []):
                if rd != i:
                    t.deps.add(rd)
            self.last_write[b] = i
            self.readers[b] = []
        if self.floor:
            for eng_, j in self.floor.items():
                if eng_ != t.eng:
                    t.deps.add(j)
        self.tasks.append(t)
        return i

    def barrier(self):
        fl = {}
        for i, t in enumerate(self.tasks):
            fl[t.eng] = i
        self.floor = fl

    def finalize(self):
        tasks = self.tasks
        for t in tasks:
            for j in t.deps:
                if tasks[j].eng != t.eng or t.eng != "T":
                    tasks[j].flag = True
        cnt = {}
        for t in tasks:
            if t.flag:
                cnt[t.eng] = cnt.get(t.eng, 0) + t.inc
            t.cum = cnt.get(t.eng, 0)
        for t in tasks:
            for j in t.deps:
                tj = tasks[j]
                if tj.eng == t.eng and t.eng == "T":
                    continue
                if tj.eng == t.eng and tj.cum >= t.cum:
                    pass
                if tj.flag and tj.cum > t.waits.get(tj.eng, -1):
                    t.waits[tj.eng] = tj.cum
        self.final_counts = cnt


def _tok_major(tab):
    return np.ascontiguousarray(
        tab.reshape(NT, 128, 64).transpose(1, 0, 2).reshape(128, NT * 64))


def make_tables():
    t64 = np.arange(T, dtype=np.float64)
    k = np.arange(0, D, 2, dtype=np.float64)
    inv = 10000.0 ** (-(k / D))
    fr = np.outer(t64, inv)
    f2 = np.concatenate([fr, fr], axis=1)
    sign = np.where(np.arange(D) < 32, -1.0, 1.0)[None, :]
    C2 = np.cos(f2)
    S2 = sign * np.sin(f2)
    half = np.arange(0, D, 2, dtype=np.float64)
    invf = 10000.0 ** (-(half / D))
    frf = np.outer(t64, invf)
    ff = np.concatenate([frf, frf], axis=1)
    base = (half + 0.4 * D) / (1.4 * D)
    power = (t64 - T // 2) / 256.0
    sc = base[None, :] ** power[:, None]
    sc2 = np.concatenate([sc, sc], axis=1)
    CXq = np.cos(ff) * sc2
    SXq = sign * np.sin(ff) * sc2
    CXk = np.cos(ff) / sc2
    SXk = sign * np.sin(ff) / sc2
    f32 = lambda a: _tok_major(a).astype(np.float32)
    return f32(C2), f32(S2), f32(CXq), f32(SXq), f32(CXk), f32(SXk)


def make_masks():
    p = np.arange(128)[:, None]
    co = np.arange(KW_O)[None, :]
    mo = [np.where((co >= p) & (co <= p + 256) & (co >= 256 - 128 * t), 0.0, NEG)
          for t in range(KB_O)]
    cf = np.arange(KW_F)[None, :]
    mf = [np.where((cf >= p) & (cf <= p + 512) & (cf >= 512 - 128 * t), 0.0, NEG)
          for t in range(KB_F)]
    return (np.concatenate(mo, axis=0).astype(BFDT),
            np.concatenate(mf, axis=0).astype(BFDT))


def permute_w1(w1):
    blocks = []
    a, g = w1[:, :2730], w1[:, 2730:]
    for c in range(NPAIR):
        ab = np.zeros((1024, 128), np.float32)
        gb = np.zeros((1024, 128), np.float32)
        n = min(128, 2730 - c * 128)
        ab[:, :n] = a[:, c * 128:c * 128 + n]
        gb[:, :n] = g[:, c * 128:c * 128 + n]
        blocks += [ab, gb]
    return np.concatenate(blocks, axis=1)


def permute_b1(b1):
    a, g = b1[:2730], b1[2730:]
    cols = np.zeros((128, 2 * NPAIR), np.float32)
    for c in range(NPAIR):
        n = min(128, 2730 - c * 128)
        cols[:n, 2 * c] = a[c * 128:c * 128 + n]
        cols[:n, 2 * c + 1] = g[c * 128:c * 128 + n]
    return cols


def pad_w2(w2):
    out = np.zeros((W2R, 1024), np.float32)
    out[:2730] = w2
    return out



def _ld_chunks(q, dst, src_ap, nchunks, ncols):
    d = bass.AP(dst, 0, [[dst.shape[1], 128], [ncols, nchunks], [1, ncols]])
    s = src_ap.rearrange("(kc p) n -> p kc n", p=128)
    return q.dma_start(d, s)


def build(nb=NB):
    nc = bass.Bass("TRN2", target_bir_lowering=False)
    dram = {}

    def din(name, shape, dt=F32):
        dram[name] = nc.dram_tensor(name, shape, dt, kind="ExternalInput").ap()
        return dram[name]

    x_d = din("x", [nb * T, C])
    wqt_d = din("wqt", [C, 3 * C])
    wot_d = din("wot", [C, C])
    fqkv_d = din("fqkv", [2 * C, 3 * CF])
    fwo_d = din("fwo", [2 * CF, C])
    fw1_d = din("fw1", [2 * C, W1C])
    fw2_d = din("fw2", [2 * W2R, C])
    fb1_d = din("fb1", [2 * 128, 2 * NPAIR])
    fb2_d = din("fb2", [2, C])
    lgb_d = din("lgb", [2 * 128, C])
    lbb_d = din("lbb", [2 * 128, C])
    qsb_d = din("qsb", [2 * 128, D])
    ksb_d = din("ksb", [2 * 128, D])
    tc2_d = din("tc2", [128, NT * D]); ts2_d = din("ts2", [128, NT * D])
    tcx_d = {("q", "c"): din("tcxq", [128, NT * D]),
             ("q", "s"): din("tsxq", [128, NT * D]),
             ("k", "c"): din("tcxk", [128, NT * D]),
             ("k", "s"): din("tsxk", [128, NT * D])}
    mko_d = din("mko", [KB_O * 128, KW_O], BF16)
    mkf_d = din("mkf", [KB_F * 128, KW_F], BF16)
    id_d = din("ident", [128, 128], BF16)
    shm_d = din("shm", [128, 128], BF16)
    e127_d = din("e127", [128, 128], BF16)
    shmf_d = din("shmf", [128, 128], F32)
    e127f_d = din("e127f", [128, 128], F32)
    ones1_d = din("ones1", [1, 128], BF16)

    out_d = nc.dram_tensor("out", [nb * T, C], F32, kind="ExternalOutput").ap()
    x1p_d = nc.dram_tensor("x1p", [T, C], F32).ap()

    dbg_seen = set()

    def dbg(name, ap_fn, shape, dt, reads):
        if not DEBUG or name in dbg_seen:
            return
        dbg_seen.add(name)
        dd = nc.dram_tensor("dbg_" + name, shape, dt, kind="ExternalOutput").ap()

        def em(q):
            return [q.dma_start(dd, ap_fn())]
        P.add("Sy", em, reads=reads, writes=["dbg" + name], ndma=1, tag="dbg" + name)

    P = Prog()
    root = ExitStack()

    def mk(stack, name, shape, dt):
        return stack.enter_context(nc.sbuf_tensor(name, shape, dt))

    xf = mk(root, "xf", [128, NT * C], F32)
    Pb = mk(root, "Pb", [128, 4 * KW_O], BF16)
    PbT = mk(root, "PbT", [128, 4 * KW_O], BF16)
    ob = mk(root, "ob", [128, C], BF16)
    oT = mk(root, "oT", [128, C], BF16)
    stat = mk(root, "stat", [128, 32], F32)
    statc = [0]
    rmx = mk(root, "rmx", [128, 16], F32)
    ngm = mk(root, "ngm", [128, 16], F32)
    zc = mk(root, "zc", [128, 16], F32)
    rzc = mk(root, "rzc", [128, 16], F32)
    ssq8 = mk(root, "ssq8", [128, 16], F32)
    ssk8 = mk(root, "ssk8", [128, 16], F32)
    cq8 = mk(root, "cq8", [128, 8], F32)
    rk8 = mk(root, "rk8", [128, 16], F32)
    sq8a = mk(root, "sq8a", [128, 16], F32)
    sq8b = mk(root, "sq8b", [128, 16], F32)
    idm = mk(root, "idm", [128, 128], BF16)
    shmF = mk(root, "shmF", [128, 128], F32)
    e127F = mk(root, "e127F", [128, 128], F32)
    ones1 = mk(root, "ones1S", [1, 128], BF16)
    rsc1 = mk(root, "rsc1", [128, C], F32)
    rsc2 = mk(root, "rsc2", [128, C], F32)
    rsc4 = mk(root, "rsc4", [128, C], F32)
    shb = mk(root, "shb", [128, C], BF16)
    aT = mk(root, "aT", [128, C], BF16)
    qz = mk(root, "qz", [128, H_O * 128], BF16)

    PA = root.enter_context(nc.psum_tensor("PA", [128, 1024], F32))
    PB = root.enter_context(nc.psum_tensor("PB", [128, 1024], F32))
    PC = root.enter_context(nc.psum_tensor("PC", [128, 512], F32))
    PD = root.enter_context(nc.psum_tensor("PD", [128, 512], F32))
    PT = root.enter_context(nc.psum_tensor("PT", [128, 1024], BF16))
    PT2 = root.enter_context(nc.psum_tensor("PT2", [128, 1024], BF16))
    PTs = [(PT, "PT"), (PT2, "PT2")]
    ptc = [0]

    def pt_next():
        ptc[0] ^= 1
        return PTs[ptc[0]]

    def bankset(nm, lo, hi):
        s = []
        if lo < 512:
            s.append(nm + "0")
        if hi > 512:
            s.append(nm + "1")
        return s

    def em_ldc2(q):
        return [q.dma_start(idm[:], id_d), q.dma_start(shmF[:], shmf_d),
                q.dma_start(e127F[:], e127f_d), q.dma_start(ones1[:], ones1_d)]
    P.add("G", em_ldc2, writes=["idm", "shmF", "e127F", "ones1"], ndma=4, tag="ldc2")

    def em_qz0(v):
        return [v.memset(qz[:], 0.0)]
    P.add("V", em_qz0, writes=["qzD"], tag="qz0")

    def ln_stats(pref, src_aps, nfree, extra_rms):
        statc[0] ^= 1
        so_ = statc[0] * 16
        sname = f"stat{statc[0]}"
        sm = stat[:, so_ + 0:so_ + 1]; ssc = stat[:, so_ + 1:so_ + 2]
        t1 = stat[:, so_ + 2:so_ + 3]; aa = stat[:, so_ + 3:so_ + 4]
        bb = stat[:, so_ + 4:so_ + 5]; lnv = stat[:, so_ + 5:so_ + 6]
        r1 = stat[:, so_ + 6:so_ + 7]; nmc = stat[:, so_ + 7:so_ + 8]
        tr_ = stat[:, so_ + 8:so_ + 9]; tq = stat[:, so_ + 9:so_ + 10]
        mean = stat[:, so_ + 10:so_ + 11]
        rnames = [n for _, n, _ in src_aps]
        c1 = ((1.0 + 1.0e-6) if extra_rms else 1.0) / nfree
        c0 = 1.0e-11 if extra_rms else 1.0e-5

        def em_v1(v):
            return [v.reduce_sum(tr_ if i else sm, ap, axis=AX.X)
                    for i, (ap, _, _) in enumerate(src_aps)]
        P.add("V", em_v1, reads=rnames + [sname], writes=[sname], tag=pref + "v1")

        def em_s1(s):
            return [s.activation(rsc4[:, 0:w], ap, AF.Square,
                                 accum_out=(tq if i else ssc))
                    for i, (ap, _, w) in enumerate(src_aps)]
        P.add("S", em_s1, reads=rnames + [sname], writes=[sname, "rs4"],
              tag=pref + "s1")

        if len(src_aps) > 1:
            def em_v1b(v):
                return [v.tensor_add(sm, sm, tr_), v.tensor_add(ssc, ssc, tq)]
            P.add("V", em_v1b, reads=[sname], writes=[sname], tag=pref + "v1b")

        def em_v2(v):
            return [v.scalar_tensor_tensor(t1, sm, 1.0 / nfree, sm,
                                           ALU.mult, ALU.mult)]
        P.add("V", em_v2, reads=[sname], writes=[sname], tag=pref + "v2")

        def em_v3(v):
            return [v.tensor_sub(aa, ssc, t1)]
        P.add("V", em_v3, reads=[sname], writes=[sname], tag=pref + "v3")

        def em_v4(v):
            return [v.tensor_scalar(bb, aa, c1, c0, ALU.mult, ALU.add)]
        P.add("V", em_v4, reads=[sname], writes=[sname], tag=pref + "v4")

        def em_s2(s):
            return [s.activation(lnv, bb, AF.Ln)]
        P.add("S", em_s2, reads=[sname], writes=[sname], tag=pref + "s2")

        def em_s3(s):
            return [s.activation(r1, lnv, AF.Exp, scale=-0.5)]
        P.add("S", em_s3, reads=[sname], writes=[sname], tag=pref + "s3")

        def em_v5(v):
            r = [v.scalar_tensor_tensor(nmc, sm, -1.0 / nfree, r1,
                                        ALU.mult, ALU.mult)]
            if not extra_rms:
                r.append(v.tensor_scalar_mul(mean, sm, 1.0 / nfree))
            return r
        P.add("V", em_v5, reads=[sname], writes=[sname], tag=pref + "v5")
        return r1, nmc, mean, sname

    def attention_tile(pref, t, nheads, kb, kw, kroll, kn_pref, vroll, vn_pref,
                       nslots, vwidth, mask_sb, mask_name, o_ps_fn, o_names,
                       exp_scale, cq_fn, full_kv=False, mid_hook=None,
                       qz_off=0, qz_name="qzD", hooks=None, cq_name="cq8"):
        teff = min(t, kb - 1)
        kts = [t - (kb - 1) + j for j in range(kb)]
        pbd = 4 if kw <= 512 else 2
        if full_kv:
            kreads = ["kTF"]
            vreads = ["vF"]
            spc_full = nslots
        else:
            kreads = [kn_pref + str(kt % nslots) for kt in kts if kt >= 0]
            vreads = [vn_pref + str(kt % nslots) for kt in kts if kt >= 0]
            spc_full = nslots + kb - 1

        if kw <= 512:
            slots = [(PA, 0, ["PA0"]), (PA, 512, ["PA1"]),
                     (PB, 0, ["PB0"]), (PB, 512, ["PB1"])]
        else:
            slots = [(PA, 0, ["PA0", "PA1"]), (PB, 0, ["PB0", "PB1"])]
        depth = len(slots)

        def sc_slot(h):
            return slots[h % depth]

        def em_sc_f(h):
            sc_t, sc_o, sbanks = sc_slot(h)
            pc = h // 2

            def em_sc(te, h=h, pc=pc, sc_t=sc_t, sc_o=sc_o, qz_off=qz_off):
                r = []
                started = [False, False]
                if t >= kb - 1:
                    spc = spc_full
                    s0 = (t - (kb - 1)) % nslots
                    base = (pc * spc + s0) * 128
                    n1 = min(kw, 512)
                    r.append(te.matmul(sc_t[:, sc_o:sc_o + n1],
                                       qz[:, qz_off + h * 128:qz_off + (h + 1) * 128],
                                       kroll[:, base:base + n1],
                                       start=True, stop=False,
                                       skip_group_check=True))
                    started[(sc_o) // 512] = True
                    if kw > 512:
                        r.append(te.matmul(sc_t[:, sc_o + 512:sc_o + kw],
                                           qz[:, qz_off + h * 128:qz_off + (h + 1) * 128],
                                           kroll[:, base + 512:base + kw],
                                           start=True, stop=False,
                                           skip_group_check=True))
                        started[1] = True
                else:
                    for j, kt in enumerate(kts):
                        if kt < 0:
                            continue
                        bk = (sc_o + j * 128) // 512
                        spc = spc_full
                        r.append(te.matmul(
                            sc_t[:, sc_o + j * 128:sc_o + (j + 1) * 128],
                            qz[:, qz_off + h * 128:qz_off + (h + 1) * 128],
                            kroll[:, (pc * spc + (kt % nslots)) * 128:
                                  (pc * spc + (kt % nslots)) * 128 + 128],
                            start=not started[bk], stop=False,
                            skip_group_check=True))
                        started[bk] = True
                o = teff * kw
                n0 = min(kw, 512)
                mb = sc_o // 512
                if t >= kb - 1:
                    dcol = (kb - 1) * 128
                    r.append(te.matmul(sc_t[:, sc_o:sc_o + 128], idm[:],
                                       mask_sb[:, o:o + 128],
                                       start=not started[mb], stop=False,
                                       skip_group_check=True))
                    r.append(te.matmul(sc_t[:, sc_o + dcol:sc_o + kw], idm[:],
                                       mask_sb[:, o + dcol:o + kw],
                                       start=not started[(sc_o + dcol) // 512],
                                       stop=True, skip_group_check=True))
                    return r
                r.append(te.matmul(sc_t[:, sc_o:sc_o + n0], idm[:],
                                   mask_sb[:, o:o + n0],
                                   start=not started[mb], stop=(kw <= 512),
                                   skip_group_check=True))
                if kw > 512:
                    r.append(te.matmul(sc_t[:, sc_o + 512:sc_o + kw], idm[:],
                                       mask_sb[:, o + 512:o + kw],
                                       start=not started[mb + 1], stop=True,
                                       skip_group_check=True))
                return r
            P.add("T", em_sc, reads=[qz_name, mask_name, "idm"] + kreads,
                  writes=sbanks, tag=f"{pref}sc{t}_{h}")

        def em_rmng_f(h):
            sc_t, sc_o, sbanks = sc_slot(h)

            def em_rm(v, h=h, sc_t=sc_t, sc_o=sc_o):
                return [v.reduce_max(rmx[:, h:h + 1], sc_t[:, sc_o:sc_o + kw],
                                     axis=AX.X)]
            P.add("V", em_rm, reads=sbanks, writes=["rmx"], tag=f"{pref}rm{t}_{h}")

            def em_ng(v, h=h):
                if cq_fn is None:
                    return [v.tensor_scalar_mul(ngm[:, h:h + 1], rmx[:, h:h + 1],
                                                -exp_scale)]
                return [v.scalar_tensor_tensor(ngm[:, h:h + 1], rmx[:, h:h + 1],
                                               -1.0, cq_fn(h), ALU.mult, ALU.mult)]
            P.add("V", em_ng, reads=["rmx"] + ([cq_name] if cq_fn else []),
                  writes=["ngm"], tag=f"{pref}ng{t}_{h}")

        def em_ex_f(h):
            sc_t, sc_o, sbanks = sc_slot(h)
            par = h % pbd
            po = par * kw

            def em_ex(s, h=h, sc_t=sc_t, sc_o=sc_o, po=po):
                scale = exp_scale if cq_fn is None else cq_fn(h)
                return [s.activation(Pb[:, po:po + kw], sc_t[:, sc_o:sc_o + kw],
                                     AF.Exp, bias=ngm[:, h:h + 1], scale=scale,
                                     accum_out=zc[:, h:h + 1])]
            P.add("S", em_ex, reads=sbanks + ["ngm"] + ([cq_name] if cq_fn else []),
                  writes=[f"Pbb{par}", "zc"], tag=f"{pref}ex{t}_{h}")

        def em_ptpe_f(h):
            par = h % pbd
            po = par * kw
            pt_, ptn = pt_next()

            def em_pt(te, po=po, pt_=pt_):
                return [te.transpose(pt_[:, j * 128:(j + 1) * 128],
                                     Pb[:, po + j * 128:po + (j + 1) * 128], idm[:])
                        for j, kt in enumerate(kts) if kt >= 0]
            P.add("T", em_pt, reads=[f"Pbb{par}", "idm"], writes=[ptn],
                  tag=f"{pref}pt{t}_{h}")

            def em_pe(v, po=po, pt_=pt_):
                return [v.tensor_copy(PbT[:, po:po + kw], pt_[:, 0:kw])]
            P.add("V", em_pe, reads=[ptn], writes=[f"PbTb{par}"],
                  tag=f"{pref}pe{t}_{h}")

        def em_pv_f(h):
            par = h % pbd
            po = par * kw

            def em_pv(te, h=h, po=po):
                r = []
                first = True
                for j, kt in enumerate(kts):
                    if kt < 0:
                        continue
                    r.append(te.matmul(
                        o_ps_fn(h),
                        PbT[:, po + j * 128:po + (j + 1) * 128],
                        vroll[:, (kt % nslots) * vwidth + h * 64:
                              (kt % nslots) * vwidth + h * 64 + 64],
                        start=first, stop=(j == kb - 1), skip_group_check=True))
                    first = False
                return r
            P.add("T", em_pv, reads=[f"PbTb{par}"] + vreads, writes=o_names,
                  tag=f"{pref}pv{t}_{h}")

        for h in range(min(depth, nheads)):
            em_sc_f(h)
        for h in range(min(pbd, nheads)):
            em_rmng_f(h)
            em_ex_f(h)
        for h in range(nheads):
            if h + depth < nheads:
                em_sc_f(h + depth)
            if h + pbd < nheads:
                em_rmng_f(h + pbd)
            em_ptpe_f(h)
            if h >= 1:
                em_pv_f(h - 1)
            if h + pbd < nheads:
                em_ex_f(h + pbd)
            if mid_hook is not None and h == nheads // 2:
                mid_hook()
            if hooks is not None and h in hooks:
                for fn in hooks[h]:
                    fn()
        em_pv_f(nheads - 1)

        def em_rz(v):
            return [v.reciprocal(rzc[:, 0:nheads], zc[:, 0:nheads])]
        P.add("V", em_rz, reads=["zc"], writes=["rzc"], tag=f"{pref}rz{t}")

    def stageB(b):
        st_ = ExitStack()
        WQ = mk(st_, f"WQ{b}", [128, 8 * 3 * C], BF16)
        WO = mk(st_, f"WO{b}", [128, 8 * C], BF16)
        kroll = mk(st_, f"kro{b}", [128, 8 * 6 * 128], BF16)
        vroll = mk(st_, f"vro{b}", [128, 4 * C], BF16)
        qnb = mk(st_, f"qnb{b}", [128, C], BF16)
        knb = mk(st_, f"knb{b}", [128, C], BF16)
        xtile = mk(st_, f"xt{b}", [128, 2 * C], F32)
        x1sb = mk(st_, f"x1b{b}", [128, C], F32)
        tC2 = mk(st_, f"tC2{b}", [128, NT * D], F32)
        tS2 = mk(st_, f"tS2{b}", [128, NT * D], F32)
        mko = mk(st_, f"mko{b}", [128, KB_O * KW_O], BF16)

        def em_w(q):
            return [_ld_chunks(q, WQ, wqt_d, 8, 3 * C),
                    _ld_chunks(q, WO, wot_d, 8, C)]
        P.add("G", em_w, writes=["WQ", "WO"], ndma=2, tag=f"b{b}w")

        def em_t(q):
            return [q.dma_start(tC2[:], tc2_d), q.dma_start(tS2[:], ts2_d),
                    _ld_chunks(q, mko, mko_d, KB_O, KW_O)]
        P.add("Sy", em_t, writes=["tC2", "tS2", "mko"], ndma=3, tag=f"b{b}t")

        def partA(t):
            xcur = xtile[:, (t % 2) * C:(t % 2) * C + C]
            xname = f"xt{t % 2}"
            def em_x(q, t=t, xcur=xcur):
                return [q.dma_start(xcur, x_d[b * T + t * 128:b * T + (t + 1) * 128, :])]
            P.add("Sy", em_x, writes=[xname], ndma=1, tag=f"b{b}x{t}")

            comb, nmc, _, sn1 = ln_stats(f"b{b}n{t}", [(xcur, xname, C)], C, True)

            def em_ap(v, xcur=xcur, comb=comb, nmc=nmc):
                return [v.tensor_scalar(shb[:], xcur, comb, nmc, ALU.mult, ALU.add)]
            P.add("V", em_ap, reads=[xname, sn1], writes=["shbb"], tag=f"b{b}ap{t}")
            if b == 0 and t < 2:
                dbg(f"xn2_{t}", lambda: shb[:], [128, C], BF16, ["shbb"])

            pt_, ptn = pt_next()

            def em_tr(te, pt_=pt_):
                return [te.transpose(pt_[:, kc * 128:(kc + 1) * 128],
                                     shb[:, kc * 128:(kc + 1) * 128], idm[:])
                        for kc in range(8)]
            P.add("T", em_tr, reads=["shbb", "idm"], writes=[ptn], tag=f"b{b}tr{t}")

            def em_te(v, pt_=pt_):
                return [v.tensor_copy(aT[:], pt_[:])]
            P.add("V", em_te, reads=[ptn], writes=["aTb"], tag=f"b{b}te{t}")

        partA(0)
        for t in range(NT):
            xcur = xtile[:, (t % 2) * C:(t % 2) * C + C]
            xname = f"xt{t % 2}"

            for halves, nms, nc0 in (((PA[:, 0:512], PA[:, 512:1024]), ["PA0", "PA1"], 0),
                                     ((PB[:, 0:512], PB[:, 512:1024]), ["PB0", "PB1"], 2),
                                     ((PC[:, 0:512], PD[:, 0:512]), ["PC", "PD"], 4)):
                def em_mm(te, halves=halves, nc0=nc0):
                    r = []
                    for half in range(2):
                        ncx = nc0 + half
                        for kc in range(8):
                            r.append(te.matmul(
                                halves[half],
                                aT[:, kc * 128:(kc + 1) * 128],
                                WQ[:, kc * 3 * C + ncx * 512:kc * 3 * C + ncx * 512 + 512],
                                start=(kc == 0), stop=(kc == 7)))
                    return r
                P.add("T", em_mm, reads=["aTb", "WQ"], writes=nms,
                      tag=f"b{b}mm{t}{nms[0]}")

            for src, snm, dstb, dnm in ((PA, "PA", qnb, "qnb"), (PB, "PB", knb, "knb")):
                def em_rp(v, src=src, t=t):
                    ap3 = bass.AP(src, 0, [[1024, 128], [64, 16], [1, 64]])
                    rot = bass.AP(src, 32, [[1024, 128], [64, 16], [-32, 2], [1, 32]])
                    c_ap = bass.AP(tC2, t * 64, [[NT * D, 128], [0, 16], [1, 64]])
                    s_ap = bass.AP(tS2, t * 64, [[NT * D, 128], [0, 16], [32, 2], [1, 32]])
                    r1_ = bass.AP(rsc1, 0, [[C, 128], [64, 16], [1, 64]])
                    r2_ = bass.AP(rsc2, 0, [[C, 128], [64, 16], [32, 2], [1, 32]])
                    return [v.tensor_tensor(r1_, ap3, c_ap, ALU.mult),
                            v.tensor_tensor(r2_, rot, s_ap, ALU.mult)]
                P.add("V", em_rp, reads=[snm + "0", snm + "1", "tC2", "tS2"],
                      writes=["rs1", "rs2"], tag=f"b{b}rp{t}{dnm}")

                def em_rp2(v, dstb=dstb):
                    return [v.tensor_add(dstb[:], rsc1[:], rsc2[:])]
                P.add("V", em_rp2, reads=["rs1", "rs2"], writes=[dnm],
                      tag=f"b{b}rpa{t}{dnm}")
                if b == 0 and t < 2:
                    dbg(f"{dnm}_{t}", (lambda dstb=dstb: dstb[:]), [128, C], BF16, [dnm])

            def em_ve(s, t=t):
                return [s.copy(vroll[:, (t % 4) * C:(t % 4) * C + 512], PC[:]),
                        s.copy(vroll[:, (t % 4) * C + 512:(t % 4) * C + C], PD[:])]
            P.add("S", em_ve, reads=["PC", "PD"], writes=[f"vr{t % 4}"],
                  tag=f"b{b}ve{t}")
            if b == 0 and t < 2:
                dbg(f"v_{t}", (lambda t=t: vroll[:, (t % 4) * C:(t % 4) * C + C]),
                    [128, C], BF16, [f"vr{t % 4}"])

            pt_, ptn = pt_next()

            def em_qt(te, pt_=pt_):
                return [te.transpose(
                    pt_[(h % 2) * 64:(h % 2) * 64 + 64, (h // 2) * 128:(h // 2) * 128 + 128],
                    qnb[:, h * 64:h * 64 + 64], idm[:]) for h in range(H_O)]
            P.add("T", em_qt, reads=["qnb", "idm"], writes=[ptn], tag=f"b{b}qt{t}")

            def em_qe(v, pt_=pt_):
                d_e = bass.AP(qz, 0, [[H_O * 128, 64], [256, 8], [1, 128]])
                s_e = bass.AP(pt_, 0, [[1024, 64], [128, 8], [1, 128]])
                d_o = bass.AP(qz, 64 * (H_O * 128) + 128, [[H_O * 128, 64], [256, 8], [1, 128]])
                s_o = bass.AP(pt_, 64 * 1024, [[1024, 64], [128, 8], [1, 128]])
                return [v.tensor_copy(d_e, s_e), v.tensor_copy(d_o, s_o)]
            P.add("V", em_qe, reads=[ptn], writes=["qzD"], tag=f"b{b}qe{t}")
            if b == 0 and t < 2:
                dbg(f"qz_{t}", lambda: qz[:], [128, H_O * 128], BF16, ["qzD"])

            pt_, ptn = pt_next()

            def em_kt(te, pt_=pt_):
                return [te.transpose(pt_[:, pc * 128:(pc + 1) * 128],
                                     knb[:, pc * 128:(pc + 1) * 128], idm[:])
                        for pc in range(8)]
            P.add("T", em_kt, reads=["knb", "idm"], writes=[ptn], tag=f"b{b}kt{t}")

            def em_ke(v, t=t, pt_=pt_):
                ncp = 2 if (t % 4) < 2 else 1
                d = bass.AP(kroll, (t % 4) * 128,
                            [[8 * 6 * 128, 128], [6 * 128, 8], [4 * 128, ncp], [1, 128]])
                s_ = bass.AP(pt_, 0, [[1024, 128], [128, 8], [0, ncp], [1, 128]])
                return [v.tensor_copy(d, s_)]
            P.add("V", em_ke, reads=[ptn], writes=[f"kr{t % 4}"], tag=f"b{b}ke{t}")
            if b == 0 and t < 2:
                dbg(f"kr_{t}", (lambda t=t: bass.AP(kroll, (t % 4) * 128,
                    [[8 * 4 * 128, 128], [4 * 128, 8], [1, 128]])),
                    [128, 8 * 128], BF16, [f"kr{t % 4}"])

            attention_tile(f"b{b}a", t, H_O, KB_O, KW_O, kroll, "kr", vroll, "vr",
                           4, C, mko, "mko",
                           (lambda h: PC[:, (h % 8) * 64:(h % 8) * 64 + 64] if h < 8
                            else PD[:, (h - 8) * 64:(h - 8) * 64 + 64]),
                           ["PC", "PD"], 0.125, None,
                           mid_hook=((lambda t=t: partA(t + 1)) if t + 1 < NT else None))

            def em_oe(v):
                rz0 = bass.AP(rzc, 0, [[16, 128], [1, 8], [0, 64]])
                rz1 = bass.AP(rzc, 8, [[16, 128], [1, 8], [0, 64]])
                p3a = bass.AP(PC, 0, [[512, 128], [64, 8], [1, 64]])
                p3b = bass.AP(PD, 0, [[512, 128], [64, 8], [1, 64]])
                o3a = bass.AP(ob, 0, [[C, 128], [64, 8], [1, 64]])
                o3b = bass.AP(ob, 512, [[C, 128], [64, 8], [1, 64]])
                return [v.tensor_tensor(o3a, p3a, rz0, ALU.mult),
                        v.tensor_tensor(o3b, p3b, rz1, ALU.mult)]
            P.add("V", em_oe, reads=["PC", "PD", "rzc"], writes=["obb"],
                  tag=f"b{b}oe{t}")
            if b == 0 and t < 3:
                dbg(f"ob_{t}", lambda: ob[:], [128, C], BF16, ["obb"])

            pt_, ptn = pt_next()

            def em_ot(te, pt_=pt_):
                return [te.transpose(pt_[:, kc * 128:(kc + 1) * 128],
                                     ob[:, kc * 128:(kc + 1) * 128], idm[:])
                        for kc in range(8)]
            P.add("T", em_ot, reads=["obb", "idm"], writes=[ptn], tag=f"b{b}ot{t}")

            def em_ote(v, pt_=pt_):
                return [v.tensor_copy(oT[:], pt_[:])]
            P.add("V", em_ote, reads=[ptn], writes=["oTb"], tag=f"b{b}ote{t}")

            def em_wo(te):
                r = []
                for nh in range(2):
                    for kc in range(8):
                        r.append(te.matmul(
                            PB[:, nh * 512:nh * 512 + 512],
                            oT[:, kc * 128:(kc + 1) * 128],
                            WO[:, kc * C + nh * 512:kc * C + nh * 512 + 512],
                            start=(kc == 0), stop=(kc == 7)))
                return r
            P.add("T", em_wo, reads=["oTb", "WO"], writes=["PB0", "PB1"],
                  tag=f"b{b}wo{t}")

            def em_x1(v, xcur=xcur):
                return [v.tensor_add(x1sb[:, 0:512], PB[:, 0:512], xcur[:, 0:512]),
                        v.tensor_add(x1sb[:, 512:1024], PB[:, 512:1024],
                                     xcur[:, 512:1024])]
            P.add("V", em_x1, reads=["PB0", "PB1", xname], writes=["x1s"],
                  tag=f"b{b}x1{t}")
            if b == 0 and t < 3:
                dbg(f"x1_{t}", lambda: x1sb[:], [128, C], F32, ["x1s"])

            def em_pk(q, t=t):
                return [q.dma_start(x1p_d[t * 128:(t + 1) * 128, :], x1sb[:])]
            P.add("Sy", em_pk, reads=["x1s"], writes=[f"x1p{t}"], ndma=1,
                  tag=f"b{b}pk{t}")

            comb2, nmc2, _, sn2 = ln_stats(f"b{b}h{t}", [(x1sb[:], "x1s", C)], C, False)

            def em_h0(v, t=t, comb2=comb2, nmc2=nmc2):
                return [v.tensor_scalar(xf[:, t * C:(t + 1) * C], x1sb[:], comb2, nmc2,
                                        ALU.mult, ALU.add)]
            P.add("V", em_h0, reads=["x1s", sn2], writes=[f"xf{t}"],
                  tag=f"b{b}h0{t}")
            if b == 0 and t < 2:
                dbg(f"h0_{t}", (lambda t=t: xf[:, t * C:(t + 1) * C]), [128, C], F32,
                    [f"xf{t}"])

        P.barrier()
        st_.close()

    def stageD_layer(b, l):
        st_ = ExitStack()
        FQ = mk(st_, f"FQ{b}{l}", [128, 8 * 3 * CF], BF16)
        FW = mk(st_, f"FW{b}{l}", [128, 4 * C], BF16)
        mkf = mk(st_, f"mkf{b}{l}", [128, KB_F * KW_F], BF16)
        xfTr = mk(st_, f"xfTr{b}{l}", [128, 8 * 258], BF16)
        shb2 = mk(st_, f"shb2{b}{l}", [128, C], BF16)
        qnF = mk(st_, f"qnF{b}{l}", [128, NT * CF], BF16)
        kTF = mk(st_, f"kTF{b}{l}", [128, 4 * NT * 128], BF16)
        vF = mk(st_, f"vF{b}{l}", [128, NT * CF], BF16)
        knbf = mk(st_, f"knf{b}{l}", [128, 2 * CF], BF16)
        rsc3 = rsc4
        qsB = mk(st_, f"qsB{b}{l}", [128, D], F32)
        ksB = mk(st_, f"ksB{b}{l}", [128, D], F32)
        cqF = mk(st_, f"cqF{b}{l}", [128, NT * 8], F32)
        bT = {}
        for qk in ("q", "k"):
            for cs in ("c", "s"):
                bT[qk, cs] = mk(st_, f"b{qk}{cs}{b}{l}", [128, NT * D], F32)

        def em_w(q):
            return [_ld_chunks(q, FQ, fqkv_d[l * C:(l + 1) * C, :], 8, 3 * CF),
                    _ld_chunks(q, FW, fwo_d[l * CF:(l + 1) * CF, :], 4, C)]
        P.add("G", em_w, writes=["FQ", "FW"], ndma=2, tag=f"d{b}{l}w")

        def em_tl(q):
            r = [q.dma_start(qsB[:], qsb_d[l * 128:(l + 1) * 128, :]),
                 q.dma_start(ksB[:], ksb_d[l * 128:(l + 1) * 128, :]),
                 _ld_chunks(q, mkf, mkf_d, KB_F, KW_F)]
            for qk in ("q", "k"):
                for cs in ("c", "s"):
                    r.append(q.dma_start(bT[qk, cs][:], tcx_d[qk, cs]))
            return r
        P.add("Sy", em_tl, writes=["qsB", "ksB", "bT", "mkf"], ndma=7,
              tag=f"d{b}{l}tl")

        def em_xz(v):
            zp = bass.AP(xfTr, 0, [[8 * 258, 128], [258, 8], [129, 2], [1, 1]])
            return [v.memset(zp, 0.0)]
        P.add("V", em_xz, writes=["xfTs0", "xfTs1"], tag=f"d{b}{l}xz")

        def em_bt(v):
            r = []
            for qk, scl in (("q", qsB), ("k", ksB)):
                tC = bT[qk, "c"]; tS = bT[qk, "s"]
                c3 = bass.AP(tC, 0, [[NT * D, 128], [D, NT], [1, D]])
                sc3 = bass.AP(scl, 0, [[D, 128], [0, NT], [1, D]])
                r.append(v.tensor_tensor(c3, c3, sc3, ALU.mult))
                s4 = bass.AP(tS, 0, [[NT * D, 128], [D, NT], [32, 2], [1, 32]])
                scr = bass.AP(scl, 32, [[D, 128], [0, NT], [-32, 2], [1, 32]])
                r.append(v.tensor_tensor(s4, s4, scr, ALU.mult))
            return r
        P.add("V", em_bt, reads=["bT", "qsB", "ksB"], writes=["bT"], tag=f"d{b}{l}bt")

        def ph1a(t):
            sl = t % 2
            shbc = shb if sl == 0 else shb2
            shbn = "shbb" if sl == 0 else "shbb2"
            xfn = f"xfTs{sl}"

            def em_xc(s, t=t, shbc=shbc):
                return [s.copy(shbc[:], xf[:, t * C:(t + 1) * C])]
            P.add("S", em_xc, reads=[f"xf{t}"], writes=[shbn], tag=f"d{b}{l}xc{t}")

            pt_, ptn = pt_next()

            def em_xt(te, shbc=shbc, pt_=pt_):
                return [te.transpose(pt_[:, kc * 128:(kc + 1) * 128],
                                     shbc[:, kc * 128:(kc + 1) * 128], idm[:])
                        for kc in range(8)]
            P.add("T", em_xt, reads=[shbn, "idm"], writes=[ptn], tag=f"d{b}{l}xt{t}")

            def em_xe(v, sl=sl, pt_=pt_):
                d = bass.AP(xfTr, sl * 129 + 1, [[8 * 258, 128], [258, 8], [1, 128]])
                s_ = bass.AP(pt_, 0, [[1024, 128], [128, 8], [1, 128]])
                d2 = bass.AP(xfTr, (1 - sl) * 129, [[8 * 258, 128], [258, 8], [1, 1]])
                s2 = bass.AP(pt_, 127, [[1024, 128], [128, 8], [1, 1]])
                return [v.tensor_copy(d, s_), v.tensor_copy(d2, s2)]
            P.add("V", em_xe, reads=[ptn], writes=[xfn, f"xfTs{1 - sl}"],
                  tag=f"d{b}{l}xe{t}")

        def ph1b(t):
            sl = t % 2
            xfn = f"xfTs{sl}"

            def mmgrp(ncx, pt_t, pt_o, pname):
                def em(te, ncx=ncx, pt_t=pt_t, pt_o=pt_o, sl=sl):
                    r = []
                    for kc in range(8):
                        off = sl * 129 + (1 if kc < 4 else 0)
                        r.append(te.matmul(
                            pt_t[:, pt_o:pt_o + 512],
                            xfTr[:, kc * 258 + off:kc * 258 + off + 128],
                            FQ[:, kc * 3 * CF + ncx * 512:kc * 3 * CF + ncx * 512 + 512],
                            start=(kc == 0), stop=(kc == 7)))
                    return r
                P.add("T", em, reads=[xfn, "FQ"], writes=[pname],
                      tag=f"d{b}{l}mm{ncx}{t}")
            mmgrp(1, PA, 512, "PA1")
            mmgrp(0, PA, 0, "PA0")
            mmgrp(2, PB, 0, "PB0")
            qp_t, qp_o, qp_n, qp_w = PA, 0, "PA0", 1024
            kp_t, kp_o, kp_n, kp_w = PA, 512, "PA1", 1024
            vp_t, vp_o, vp_n, vp_w = PB, 0, "PB0", 1024
            sqs = rsc2 if sl == 0 else rsc4
            sqn_ = "rs2" if sl == 0 else "rs4"
            po8 = sl * 8

            def em_sqk(s, kp_t=kp_t, kp_o=kp_o, sqs=sqs):
                return [s.activation(sqs[:, 512:1024], kp_t[:, kp_o:kp_o + 512],
                                     AF.Square)]
            P.add("S", em_sqk, reads=[kp_n], writes=[sqn_], tag=f"d{b}{l}sqk{t}")

            def em_sqq(s, qp_t=qp_t, qp_o=qp_o, sqs=sqs):
                return [s.activation(sqs[:, 0:512], qp_t[:, qp_o:qp_o + 512],
                                     AF.Square)]
            P.add("S", em_sqq, reads=[qp_n], writes=[sqn_], tag=f"d{b}{l}sqq{t}")

            def em_rqk(v, sqs=sqs, po8=po8):
                k3 = bass.AP(sqs, 512, [[C, 128], [64, 8], [1, 64]])
                return [v.reduce_sum(ssk8[:, po8:po8 + 8], k3, axis=AX.X)]
            P.add("V", em_rqk, reads=[sqn_], writes=[f"ssk{sl}"],
                  tag=f"d{b}{l}rqk{t}")

            def em_rqq(v, sqs=sqs, po8=po8):
                a3 = bass.AP(sqs, 0, [[C, 128], [64, 8], [1, 64]])
                return [v.reduce_sum(ssq8[:, po8:po8 + 8], a3, axis=AX.X)]
            P.add("V", em_rqq, reads=[sqn_], writes=[f"ssq{sl}"],
                  tag=f"d{b}{l}rqq{t}")

            def em_rpq(v, t=t, qp_t=qp_t, qp_o=qp_o, qp_w=qp_w):
                ap3 = bass.AP(qp_t, qp_o, [[qp_w, 128], [64, 8], [1, 64]])
                rot = bass.AP(qp_t, qp_o + 32, [[qp_w, 128], [64, 8], [-32, 2], [1, 32]])
                c_ap = bass.AP(bT["q", "c"], t * 64, [[NT * D, 128], [0, 8], [1, 64]])
                s_ap = bass.AP(bT["q", "s"], t * 64, [[NT * D, 128], [0, 8], [32, 2], [1, 32]])
                r1_ = bass.AP(rsc1, 0, [[C, 128], [64, 8], [1, 64]])
                r2_ = bass.AP(rsc1, 512, [[C, 128], [64, 8], [32, 2], [1, 32]])
                return [v.tensor_tensor(r1_, ap3, c_ap, ALU.mult),
                        v.tensor_tensor(r2_, rot, s_ap, ALU.mult)]
            P.add("V", em_rpq, reads=[qp_n, "bT"], writes=["rs1"],
                  tag=f"d{b}{l}rpq{t}")

            def em_rpq2(v, t=t):
                return [v.tensor_add(qnF[:, t * CF:(t + 1) * CF], rsc1[:, 0:512],
                                     rsc1[:, 512:1024])]
            P.add("V", em_rpq2, reads=["rs1"], writes=["qnF"], tag=f"d{b}{l}rpq2{t}")

            def em_snk(s, po8=po8):
                return [s.activation(sq8b[:, po8:po8 + 8], ssk8[:, po8:po8 + 8],
                                     AF.Ln)]
            P.add("S", em_snk, reads=[f"ssk{sl}"], writes=[f"sq8k{sl}"],
                  tag=f"d{b}{l}snk{t}")

            def em_snq(s, po8=po8):
                return [s.activation(sq8a[:, po8:po8 + 8], ssq8[:, po8:po8 + 8],
                                     AF.Ln)]
            P.add("S", em_snq, reads=[f"ssq{sl}"], writes=[f"sq8q{sl}"],
                  tag=f"d{b}{l}snq{t}")

            def em_rnk(s, po8=po8):
                return [s.activation(rk8[:, po8:po8 + 8], sq8b[:, po8:po8 + 8],
                                     AF.Exp, scale=-0.5)]
            P.add("S", em_rnk, reads=[f"sq8k{sl}"], writes=[f"rk{sl}"],
                  tag=f"d{b}{l}rnk{t}")

            def em_rnq(s, t=t, po8=po8):
                return [s.activation(cqF[:, t * 8:t * 8 + 8], sq8a[:, po8:po8 + 8],
                                     AF.Exp, scale=-0.5)]
            P.add("S", em_rnq, reads=[f"sq8q{sl}"], writes=["cqF"],
                  tag=f"d{b}{l}rnq{t}")

            def em_rn2k(v, po8=po8):
                return [v.tensor_scalar_min(rk8[:, po8:po8 + 8],
                                            rk8[:, po8:po8 + 8], 1.0e12)]
            P.add("V", em_rn2k, reads=[f"rk{sl}"], writes=[f"rk{sl}"],
                  tag=f"d{b}{l}rn2k{t}")

            def em_rn2q(v, t=t):
                return [v.tensor_scalar(cqF[:, t * 8:t * 8 + 8],
                                        cqF[:, t * 8:t * 8 + 8], 1.0e12, 8.0,
                                        ALU.min, ALU.mult)]
            P.add("V", em_rn2q, reads=["cqF"], writes=["cqF"],
                  tag=f"d{b}{l}rn2q{t}")


            def em_rpk(v, kp_t=kp_t, kp_o=kp_o, kp_w=kp_w, po8=po8):
                k3 = bass.AP(kp_t, kp_o, [[kp_w, 128], [64, 8], [1, 64]])
                rk_ap = bass.AP(rk8, po8, [[16, 128], [1, 8], [0, 64]])
                k1w = bass.AP(rsc3, 0, [[C, 128], [64, 8], [1, 64]])
                return [v.tensor_tensor(k1w, k3, rk_ap, ALU.mult)]
            P.add("V", em_rpk, reads=[kp_n, f"rk{sl}"], writes=["rs4"],
                  tag=f"d{b}{l}rpk{t}")

            def em_rpk2(v, t=t):
                k1r = bass.AP(rsc3, 0, [[C, 128], [64, 8], [1, 64]])
                rot = bass.AP(rsc3, 32, [[C, 128], [64, 8], [-32, 2], [1, 32]])
                c_ap = bass.AP(bT["k", "c"], t * 64, [[NT * D, 128], [0, 8], [1, 64]])
                s_ap = bass.AP(bT["k", "s"], t * 64, [[NT * D, 128], [0, 8], [32, 2], [1, 32]])
                r1_ = bass.AP(rsc1, 0, [[C, 128], [64, 8], [1, 64]])
                r2_ = bass.AP(rsc1, 512, [[C, 128], [64, 8], [32, 2], [1, 32]])
                return [v.tensor_tensor(r1_, k1r, c_ap, ALU.mult),
                        v.tensor_tensor(r2_, rot, s_ap, ALU.mult)]
            P.add("V", em_rpk2, reads=["rs4", "bT"], writes=["rs1"],
                  tag=f"d{b}{l}rpk2{t}")

            kn_c = knbf[:, sl * CF:sl * CF + CF]
            knn = f"knf{sl}"

            def em_rpk3(v, kn_c=kn_c):
                return [v.tensor_add(kn_c, rsc1[:, 0:512], rsc1[:, 512:1024])]
            P.add("V", em_rpk3, reads=["rs1"], writes=[knn], tag=f"d{b}{l}rpk3{t}")

            def em_ve(s, t=t, vp_t=vp_t, vp_o=vp_o):
                return [s.copy(vF[:, t * CF:(t + 1) * CF], vp_t[:, vp_o:vp_o + 512])]
            P.add("S", em_ve, reads=[vp_n], writes=["vF"], tag=f"d{b}{l}ve{t}")

        def ph1c(t):
            sl = t % 2
            knn = f"knf{sl}"
            pt2_, ptn2 = pt_next()

            def em_kt(te, sl=sl, pt2_=pt2_):
                return [te.transpose(pt2_[:, pc * 128:(pc + 1) * 128],
                                     bass.AP(knbf, sl * CF + pc * 128,
                                             [[2 * CF, 128], [1, 128]]), idm[:])
                        for pc in range(4)]
            P.add("T", em_kt, reads=[knn, "idm"], writes=[ptn2], tag=f"d{b}{l}kt{t}")

            def em_ke(v, t=t, pt2_=pt2_):
                d = bass.AP(kTF, t * 128, [[4 * NT * 128, 128], [NT * 128, 4], [1, 128]])
                s_ = bass.AP(pt2_, 0, [[1024, 128], [128, 4], [1, 128]])
                return [v.tensor_copy(d, s_)]
            P.add("V", em_ke, reads=[ptn2], writes=["kTF"], tag=f"d{b}{l}ke{t}")

        def qprep(t):
            qoff = (t % 2) * 1024
            qn_ = f"qzD{t % 2}"
            pt_, ptn = pt_next()

            def em_qt(te, t=t, pt_=pt_):
                return [te.transpose(
                    pt_[(h % 2) * 64:(h % 2) * 64 + 64, (h // 2) * 128:(h // 2) * 128 + 128],
                    qnF[:, t * CF + h * 64:t * CF + h * 64 + 64], idm[:])
                    for h in range(H_F)]
            P.add("T", em_qt, reads=["qnF", "idm"], writes=[ptn], tag=f"d{b}{l}qt{t}")

            def em_qe(v, pt_=pt_, qoff=qoff):
                d_e = bass.AP(qz, qoff, [[H_O * 128, 64], [256, 4], [1, 128]])
                s_e = bass.AP(pt_, 0, [[1024, 64], [128, 4], [1, 128]])
                d_o = bass.AP(qz, 64 * (H_O * 128) + qoff + 128,
                              [[H_O * 128, 64], [256, 4], [1, 128]])
                s_o = bass.AP(pt_, 64 * 1024, [[1024, 64], [128, 4], [1, 128]])
                return [v.tensor_copy(d_e, s_e), v.tensor_copy(d_o, s_o)]
            P.add("V", em_qe, reads=[ptn], writes=[qn_], tag=f"d{b}{l}qe{t}")

        ph1a(0)
        ph1b(0)
        ph1c(0)
        ph1a(1)
        ph1b(1)
        qprep(0)
        for t in range(NT):
            hooks = {2: [], 5: []}
            if t + 1 < NT:
                hooks[2].append(lambda t=t: qprep(t + 1))
                hooks[2].append(lambda t=t: ph1c(t + 1))
            if t + 2 < NT:
                hooks[2].append(lambda t=t: ph1a(t + 2))
                hooks[5].append(lambda t=t: ph1b(t + 2))
            attention_tile(f"d{b}{l}a", t, H_F, KB_F, KW_F, kTF, None, vF, None,
                           NT, CF, mkf, "mkf",
                           (lambda h: PD[:, h * 64:h * 64 + 64]), ["PD"], 1.0,
                           (lambda h, t=t: cqF[:, t * 8 + h:t * 8 + h + 1]),
                           full_kv=True, hooks=hooks, cq_name="cqF",
                           qz_off=(t % 2) * 1024, qz_name=f"qzD{t % 2}")

            def em_oe(v):
                rz0 = bass.AP(rzc, 0, [[16, 128], [1, 8], [0, 64]])
                p3 = bass.AP(PD, 0, [[512, 128], [64, 8], [1, 64]])
                o3 = bass.AP(ob, 0, [[C, 128], [64, 8], [1, 64]])
                return [v.tensor_tensor(o3, p3, rz0, ALU.mult)]
            P.add("V", em_oe, reads=["PD", "rzc"], writes=["obb"],
                  tag=f"d{b}{l}oe{t}")

            pt2_, ptn2 = pt_next()

            def em_ot(te, pt2_=pt2_):
                return [te.transpose(pt2_[:, kc * 128:(kc + 1) * 128],
                                     ob[:, kc * 128:(kc + 1) * 128], idm[:])
                        for kc in range(4)]
            P.add("T", em_ot, reads=["obb", "idm"], writes=[ptn2], tag=f"d{b}{l}ot{t}")

            def em_ote(v, pt2_=pt2_):
                return [v.tensor_copy(oT[:, 0:512], pt2_[:, 0:512])]
            P.add("V", em_ote, reads=[ptn2], writes=["oTb"], tag=f"d{b}{l}ote{t}")

            def em_wo(te):
                r = []
                for nh, dst in ((0, PC[:, 0:512]), (1, PD[:, 0:512])):
                    for kc in range(4):
                        r.append(te.matmul(
                            dst, oT[:, kc * 128:(kc + 1) * 128],
                            FW[:, kc * C + nh * 512:kc * C + nh * 512 + 512],
                            start=(kc == 0), stop=(kc == 3)))
                return r
            P.add("T", em_wo, reads=["oTb", "FW"], writes=["PC", "PD"],
                  tag=f"d{b}{l}wo{t}")

            def em_xf(v, t=t):
                return [v.tensor_add(xf[:, t * C:t * C + 512], xf[:, t * C:t * C + 512],
                                     PC[:]),
                        v.tensor_add(xf[:, t * C + 512:(t + 1) * C],
                                     xf[:, t * C + 512:(t + 1) * C], PD[:])]
            P.add("V", em_xf, reads=["PC", "PD", f"xf{t}"], writes=[f"xf{t}"],
                  tag=f"d{b}{l}xf{t}")

        P.barrier()
        st_.close()

        st2 = ExitStack()
        shn2T = mk(st2, f"s2T{b}{l}", [128, 8 * T], BF16)
        b1s = mk(st2, f"b1s{b}{l}", [128, 2 * NPAIR], F32)
        b2b = mk(st2, f"b2b{b}{l}", [1, C], BF16)
        stp = ExitStack()
        lgB = mk(stp, f"lgB{b}{l}", [128, C], F32)
        lbB = mk(stp, f"lbB{b}{l}", [128, C], F32)
        shbP = mk(stp, f"shbP{b}{l}", [128, C], BF16)

        def em_ls(q):
            return [q.dma_start(lgB[:], lgb_d[l * 128:(l + 1) * 128, :]),
                    q.dma_start(lbB[:], lbb_d[l * 128:(l + 1) * 128, :]),
                    q.dma_start(b1s[:], fb1_d[l * 128:(l + 1) * 128, :])]
        P.add("Sy", em_ls, writes=["lgB", "lbB", "b1s"], ndma=3, tag=f"f{b}{l}ls")

        def em_lb2(q):
            return [q.dma_start(b2b[:], fb2_d[l:l + 1, :])]
        P.add("G", em_lb2, writes=["b2b"], ndma=1, tag=f"f{b}{l}b2")

        def norm_tr(t):
            shbc2 = shb if t % 2 == 0 else shbP
            shbn2 = "shbb" if t % 2 == 0 else "shbP"
            pt_, ptn = pt_next()

            def em_tr(te, pt_=pt_, shbc2=shbc2):
                return [te.transpose(pt_[:, kc * 128:(kc + 1) * 128],
                                     shbc2[:, kc * 128:(kc + 1) * 128], idm[:])
                        for kc in range(8)]
            P.add("T", em_tr, reads=[shbn2, "idm"], writes=[ptn], tag=f"f{b}{l}tr{t}")

            def em_te2(v, t=t, pt_=pt_):
                d = bass.AP(shn2T, t * 128, [[8 * T, 128], [T, 8], [1, 128]])
                s_ = bass.AP(pt_, 0, [[1024, 128], [128, 8], [1, 128]])
                return [v.tensor_copy(d, s_)]
            P.add("V", em_te2, reads=[ptn], writes=[f"s2T{t}"], tag=f"f{b}{l}te{t}")

        for t in range(NT):
            sp = PD if t % 2 == 0 else PC
            spn = "PD" if t % 2 == 0 else "PC"

            def em_sh(te, t=t, sp=sp):
                r = [te.matmul(sp[:, 0:512], shmF[:],
                               xf[:, t * C + 512:(t + 1) * C],
                               start=True, stop=(t == 0), skip_group_check=True)]
                if t > 0:
                    r.append(te.matmul(sp[:, 0:512], e127F[:],
                                       xf[:, (t - 1) * C + 512:t * C],
                                       start=False, stop=True,
                                       skip_group_check=True))
                return r
            P.add("T", em_sh,
                  reads=["shmF", "e127F", f"xf{t}"] + ([f"xf{t - 1}"] if t else [])
                  + ([f"s2T{t - 2}"] if t >= 2 else []),
                  writes=[spn], tag=f"f{b}{l}sh{t}")

            comb3, nmc3, mean3, sn3 = ln_stats(
                f"f{b}{l}n{t}",
                [(xf[:, t * C:t * C + 512], f"xf{t}", 512), (sp[:, 0:512], spn, 512)],
                C, False)

            def em_ap(v, t=t, mean3=mean3, sp=sp):
                r = []
                for half, srcp in enumerate((xf[:, t * C:t * C + 512], sp[:, 0:512])):
                    o = half * 512
                    r.append(v.scalar_tensor_tensor(rsc1[:, o:o + 512], srcp, mean3,
                                                    lgB[:, o:o + 512],
                                                    ALU.subtract, ALU.mult))
                return r
            P.add("V", em_ap, reads=[f"xf{t}", spn, sn3, "lgB"],
                  writes=["rs1"], tag=f"f{b}{l}ap{t}")

            shbc2 = shb if t % 2 == 0 else shbP
            shbn2 = "shbb" if t % 2 == 0 else "shbP"

            def em_ap2(v, comb3=comb3, shbc2=shbc2):
                return [v.scalar_tensor_tensor(shbc2[:, o:o + 512], rsc1[:, o:o + 512],
                                               comb3, lbB[:, o:o + 512],
                                               ALU.mult, ALU.add)
                        for o in (0, 512)]
            P.add("V", em_ap2, reads=["rs1", sn3, "lbB"], writes=[shbn2],
                  tag=f"f{b}{l}ap2{t}")
            if b == 0 and l == 0 and t < 2:
                dbg(f"fsh_{t}", lambda: shb[:], [128, C], BF16, ["shbb"])

            if t >= 1:
                norm_tr(t - 1)
        norm_tr(NT - 1)

        P.barrier()
        stp.close()
        x1sb2 = mk(st2, f"x1r{b}{l}", [128, 2 * C], F32)
        nspl = [max(len(s) for i, s in enumerate(SPLITS) if i % 2 == j)
                for j in range(2)]
        W1d = [mk(st2, f"W1_{b}{l}{j}", [128, 8 * nspl[j] * 256], BF16)
               for j in range(2)]
        W2d = [mk(st2, f"W2_{b}{l}{j}", [128, nspl[j] * C], BF16) for j in range(2)]
        acd = [mk(st2, f"ac{b}{l}{j}", [128, nspl[j] * 512], BF16) for j in range(2)]
        for ith, split in enumerate(SPLITS):
            npp = len(split)
            p0 = split[0]
            par3 = ith % 2
            W1t, W2t, actb = W1d[par3], W2d[par3], acd[par3]
            w1n, w2n, acn = f"W1t{par3}", f"W2t{par3}", f"actb{par3}"

            def em_w1(q, p0=p0, npp=npp, W1t=W1t, W2t=W2t):
                return [_ld_chunks(q, W1t, fw1_d[l * C:(l + 1) * C,
                                                        p0 * 256:(p0 + npp) * 256],
                                           8, npp * 256),
                        _ld_chunks(q, W2t, fw2_d[l * W2R + p0 * 128:
                                                 l * W2R + (p0 + npp) * 128, :],
                                   npp, C)]
            P.add("G", em_w1, writes=[w1n, w2n], ndma=2, tag=f"f{b}{l}w{ith}")

            for sbk in range(4):
                tok0 = sbk * 512
                for ip, pr in enumerate(split):
                    pp = PA if ip % 2 == 0 else PB
                    ppn = ["PA0", "PA1"] if ip % 2 == 0 else ["PB0", "PB1"]
                    ggs = rsc2[:, 0:512] if ip % 2 == 0 else rsc2[:, 512:1024]
                    ggn = "rs2a" if ip % 2 == 0 else "rs2b"

                    def em_m1(te, ip=ip, tok0=tok0, W1t=W1t, npp=npp, pp=pp):
                        r = []
                        for half in range(2):
                            for kc in range(8):
                                r.append(te.matmul(
                                    pp[:, half * 512:half * 512 + 512],
                                    W1t[:, kc * npp * 256 + ip * 256 + half * 128:
                                        kc * npp * 256 + ip * 256 + half * 128 + 128],
                                    bass.AP(shn2T, kc * T + tok0, [[8 * T, 128], [1, 512]]),
                                    start=(kc == 0), stop=(kc == 7)))
                        return r
                    P.add("T", em_m1,
                          reads=[w1n] + [f"s2T{sbk * 4 + j}" for j in range(4)],
                          writes=ppn, tag=f"f{b}{l}m1_{ith}_{sbk}_{ip}")

                    def em_gg2(s, pr=pr, pp=pp, ggs=ggs):
                        return [s.activation(ggs, pp[:, 512:1024], AF.Gelu,
                                             bias=b1s[:, 2 * pr + 1:2 * pr + 2])]
                    P.add("S", em_gg2, reads=[ppn[1], "b1s"], writes=[ggn],
                          tag=f"f{b}{l}gg_{ith}_{sbk}_{ip}")

                    def em_ac(v, ip=ip, pr=pr, actb=actb, pp=pp, ggs=ggs):
                        return [v.scalar_tensor_tensor(
                            actb[:, ip * 512:(ip + 1) * 512], pp[:, 0:512],
                            b1s[:, 2 * pr:2 * pr + 1], ggs,
                            ALU.add, ALU.mult)]
                    P.add("V", em_ac, reads=[ppn[0], "b1s", ggn],
                          writes=[f"{acn}_{ip}"],
                          tag=f"f{b}{l}ac_{ith}_{sbk}_{ip}")
                    if b == 0 and l == 0 and ith == 0 and sbk == 0 and ip == 0:
                        dbg("fact0", (lambda actb=actb: actb[:, 0:512]), [128, 512],
                            BF16, ["actb"])

                for st in range(4):
                    t_glob = sbk * 4 + st
                    last = (ith == len(SPLITS) - 1)
                    for nh in range(2):
                        w2dst = PC[:, 0:512] if nh == 0 else PD[:, 0:512]
                        w2nm = "PC" if nh == 0 else "PD"
                        h1 = npp // 2

                        def em_m2a(te, st=st, nh=nh, actb=actb, W2t=W2t, h1=h1,
                                   w2dst=w2dst):
                            return [te.matmul(
                                w2dst,
                                actb[:, c * 512 + st * 128:c * 512 + st * 128 + 128],
                                W2t[:, c * C + nh * 512:c * C + nh * 512 + 512],
                                start=(c == 0), stop=False,
                                skip_group_check=True) for c in range(h1)]
                        P.add("T", em_m2a,
                              reads=[w2n] + [f"{acn}_{c}" for c in range(h1)],
                              writes=[w2nm], tag=f"f{b}{l}m2a_{ith}_{sbk}_{st}{nh}")

                        def em_m2b(te, st=st, nh=nh, actb=actb, W2t=W2t, npp=npp,
                                   h1=h1, last=last, w2dst=w2dst):
                            r = []
                            for c in range(h1, npp):
                                r.append(te.matmul(
                                    w2dst,
                                    actb[:, c * 512 + st * 128:c * 512 + st * 128 + 128],
                                    W2t[:, c * C + nh * 512:c * C + nh * 512 + 512],
                                    start=False, stop=(c == npp - 1 and not last),
                                    skip_group_check=True))
                            if last:
                                r.append(te.matmul(w2dst, ones1[:],
                                                   b2b[:, nh * 512:nh * 512 + 512],
                                                   start=False, stop=True,
                                                   skip_group_check=True))
                            return r
                        P.add("T", em_m2b,
                              reads=[w2n] + [f"{acn}_{c}" for c in range(h1, npp)]
                              + (["ones1", "b2b"] if last else []),
                              writes=[w2nm], tag=f"f{b}{l}m2b_{ith}_{sbk}_{st}{nh}")

                    if not (last and l == 1):
                        def em_up(v, t_glob=t_glob):
                            return [v.tensor_add(xf[:, t_glob * C:t_glob * C + 512],
                                                 xf[:, t_glob * C:t_glob * C + 512],
                                                 PC[:]),
                                    v.tensor_add(xf[:, t_glob * C + 512:(t_glob + 1) * C],
                                                 xf[:, t_glob * C + 512:(t_glob + 1) * C],
                                                 PD[:])]
                        P.add("V", em_up, reads=["PC", "PD", f"xf{t_glob}"],
                              writes=[f"xf{t_glob}"], tag=f"f{b}{l}up{ith}_{t_glob}")
                    else:
                        xsl = (t_glob % 2) * C

                        def em_un(q, t_glob=t_glob, xsl=xsl):
                            return [q.dma_start(
                                bass.AP(x1sb2, xsl, [[2 * C, 128], [1, C]]),
                                x1p_d[t_glob * 128:(t_glob + 1) * 128, :])]
                        P.add("Sy", em_un, reads=[f"x1p{t_glob}"],
                              writes=[f"x1r{t_glob % 2}"], ndma=1,
                              tag=f"f{b}{l}un{t_glob}")

                        def em_fo(v, t_glob=t_glob, xsl=xsl):
                            return [
                                v.tensor_add(x1sb2[:, xsl:xsl + 512], PC[:],
                                             x1sb2[:, xsl:xsl + 512]),
                                v.tensor_add(x1sb2[:, xsl + 512:xsl + C], PD[:],
                                             x1sb2[:, xsl + 512:xsl + C])]
                        P.add("V", em_fo, reads=["PC", "PD", f"x1r{t_glob % 2}"],
                              writes=[f"x1r{t_glob % 2}"], tag=f"f{b}{l}fo{t_glob}")

                        def em_fo2(v, t_glob=t_glob, xsl=xsl):
                            return [v.tensor_add(x1sb2[:, xsl:xsl + 512],
                                                 x1sb2[:, xsl:xsl + 512],
                                                 xf[:, t_glob * C:t_glob * C + 512]),
                                    v.tensor_add(x1sb2[:, xsl + 512:xsl + C],
                                                 x1sb2[:, xsl + 512:xsl + C],
                                                 xf[:, t_glob * C + 512:(t_glob + 1) * C])]
                        P.add("V", em_fo2, reads=[f"x1r{t_glob % 2}", f"xf{t_glob}"],
                              writes=[f"x1r{t_glob % 2}"], tag=f"f{b}{l}fo2{t_glob}")

                        def em_od(q, t_glob=t_glob, b=b, xsl=xsl):
                            return [q.dma_start(out_d[b * T + t_glob * 128:
                                                      b * T + (t_glob + 1) * 128, :],
                                                bass.AP(x1sb2, xsl,
                                                        [[2 * C, 128], [1, C]]))]
                        P.add("Sy", em_od, reads=[f"x1r{t_glob % 2}"],
                              writes=[f"od{b}_{t_glob}"],
                              ndma=1, tag=f"f{b}{l}od{t_glob}")
        P.barrier()
        st2.close()

    for b in range(nb):
        stageB(b)
        stageD_layer(b, 0)
        stageD_layer(b, 1)

    P.finalize()

    semstack = ExitStack()
    sems = {e: semstack.enter_context(nc.semaphore(f"sem{e}"))
            for e in ("T", "V", "S", "Sy", "G")}

    with root, semstack, nc.Block() as blk:
        def run_engine(eng_name):
            def body(engine):
                floors = {}
                for t in P.tasks:
                    if t.eng != eng_name:
                        continue
                    for we, wv in sorted(t.waits.items()):
                        if floors.get(we, -1) >= wv:
                            continue
                        engine.wait_ge(sems[we], wv)
                        floors[we] = wv
                    instrs = t.emit(engine)
                    if t.flag:
                        if eng_name in DMA_ENGS:
                            assert len(instrs) * 16 == t.inc, \
                                f"{t.tag}: ndma {t.inc // 16} != {len(instrs)}"
                            for ins in instrs:
                                ins.then_inc(sems[eng_name], 16)
                        else:
                            instrs[-1].then_inc(sems[eng_name], 1)
                if eng_name == "Sy":
                    engine.wait_ge(sems["Sy"], P.final_counts.get("Sy", 0))
            return body

        blk.tensor(run_engine("T"))
        blk.vector(run_engine("V"))
        blk.scalar(run_engine("S"))
        blk.sync(run_engine("Sy"))
        blk.gpsimd(run_engine("G"))

    return nc


def kernel(L0_attn_x, L0_attn_w_qkv, L0_attn_w_out, L0_ffn_wqkv, L0_ffn_q_scale,
           L0_ffn_k_scale, L0_ffn_wo, L0_ffn_ln_g, L0_ffn_ln_b, L0_ffn_w1, L0_ffn_b1,
           L0_ffn_w2, L0_ffn_b2):
    f = np.float32
    x = np.asarray(L0_attn_x, f)
    wqt = np.ascontiguousarray(np.asarray(L0_attn_w_qkv, f).T)
    wot = np.ascontiguousarray(np.asarray(L0_attn_w_out, f).T)
    fqkv = np.ascontiguousarray(np.asarray(L0_ffn_wqkv, f).reshape(2 * C, 3 * CF))
    fwo = np.ascontiguousarray(np.asarray(L0_ffn_wo, f).reshape(2 * CF, C))
    w1 = np.asarray(L0_ffn_w1, f)
    w2 = np.asarray(L0_ffn_w2, f)
    b1 = np.asarray(L0_ffn_b1, f)
    b2v = np.ascontiguousarray(np.asarray(L0_ffn_b2, f))
    fw1 = np.ascontiguousarray(np.concatenate([permute_w1(w1[i]) for i in range(2)], axis=0))
    fw2 = np.ascontiguousarray(np.concatenate([pad_w2(w2[i]) for i in range(2)], axis=0))
    fb1 = np.ascontiguousarray(np.concatenate([permute_b1(b1[i]) for i in range(2)], axis=0))
    tile2 = lambda a: np.ascontiguousarray(
        np.concatenate([np.tile(np.asarray(a, f)[i], (128, 1)) for i in range(2)], axis=0))
    lgb = tile2(L0_ffn_ln_g)
    lbb = tile2(L0_ffn_ln_b)
    qsb = tile2(L0_ffn_q_scale)
    ksb = tile2(L0_ffn_k_scale)
    tc2, ts2v, tcxq, tsxq, tcxk, tsxk = make_tables()
    mko, mkfv = make_masks()
    ident = np.eye(128, dtype=BFDT)
    shm = np.eye(128, k=1).astype(BFDT)
    e127 = np.zeros((128, 128), BFDT)
    e127[127, 0] = 1
    ones1 = np.ones((1, 128), BFDT)

    nc = build(NB)
    shmf = np.eye(128, k=1).astype(np.float32)
    e127f = np.zeros((128, 128), np.float32)
    e127f[127, 0] = 1.0
    common = dict(shmf=shmf, e127f=e127f,
                  wqt=wqt, wot=wot, fqkv=fqkv, fwo=fwo, fw1=fw1, fw2=fw2, fb1=fb1,
                  fb2=b2v, lgb=lgb, lbb=lbb, qsb=qsb, ksb=ksb,
                  tc2=tc2, ts2=ts2v, tcxq=tcxq, tsxq=tsxq, tcxk=tcxk, tsxk=tsxk,
                  mko=mko, mkf=mkfv, ident=ident, shm=shm, e127=e127, ones1=ones1)
    in_maps = []
    for cidx in range(8):
        m = dict(common)
        m["x"] = np.ascontiguousarray(x[cidx * NB:(cidx + 1) * NB].reshape(NB * T, C))
        in_maps.append(m)
    res = run_bass_kernel_spmd(nc, in_maps, core_ids=list(range(8)))
    out = np.concatenate([np.asarray(r["out"]).reshape(NB, T, C) for r in res.results],
                         axis=0)
    return out
